# version 15
# baseline (speedup 1.0000x reference)
import os
import numpy as np
import ml_dtypes
import concourse.bass as bass
import concourse.mybir as mybir
from concourse import bass_utils

F32 = mybir.dt.float32
BF16 = mybir.dt.bfloat16
AF = mybir.ActivationFunctionType
ALU = mybir.AluOpType
AX = mybir.AxisListType

B, S, D, H, DK = 4, 4096, 2048, 16, 128
NE, MI, SI = 16, 1024, 4096
OWN = 2048
NDC = D // 128
INV_SQRT_DK = 1.0 / float(np.sqrt(DK))
NEG = -1e30
LN_EPS = 1e-5


class Buf:
    __slots__ = ("name", "writers", "readers")

    def __init__(self, name=""):
        self.name = name
        self.writers = {}
        self.readers = {}


class Dep:
    ENGS = ("t", "v", "s", "q", "g")

    def __init__(self):
        self.ops = {e: [] for e in self.ENGS}
        self.dma_count = {}
        self.nbar = 0
        self.cur_phase = "init"

    def phase(self, name):
        self.cur_phase = name

    def _record(self, eng, fn, reads, writes, dma_sem=None):
        idx = len(self.ops[eng])
        waits = {}
        pad = False

        def need(key, val):
            if key[0] == "E" and key[1] == eng:
                return
            if waits.get(key, -1) < val:
                waits[key] = val

        for b in reads:
            for k, v in b.writers.items():
                need(k, v)
                if k == ("E", eng) and v == idx - 1 and eng in ("v", "s"):
                    pad = True
        for b in writes:
            for k, v in b.writers.items():
                need(k, v)
            for k, v in b.readers.items():
                need(k, v)
        for (kind, e2), v in list(waits.items()):
            if kind == "E":
                self.ops[e2][v]["inc"] = True
        self.ops[eng].append({"fn": fn, "waits": waits, "inc": False,
                              "dma_sem": dma_sem, "pad": pad,
                              "ph": self.cur_phase})
        if dma_sem is not None:
            self.dma_count[dma_sem] = self.dma_count.get(dma_sem, 0) + 1
            val = 16 * self.dma_count[dma_sem]
            for b in writes:
                b.writers[("D", dma_sem)] = val
            for b in reads:
                b.readers[("D", dma_sem)] = val
        else:
            for b in writes:
                b.writers[("E", eng)] = idx
            for b in reads:
                b.readers[("E", eng)] = idx
        return idx

    def t(self, fn, reads=(), writes=()):
        return self._record("t", fn, reads, writes)

    def v(self, fn, reads=(), writes=()):
        return self._record("v", fn, reads, writes)

    def s(self, fn, reads=(), writes=()):
        return self._record("s", fn, reads, writes)

    def dma(self, queue, sem, fn, reads=(), writes=()):
        return self._record(queue, fn, reads, writes, dma_sem=sem)

    def seal(self, sem, bufs):
        v = 16 * self.dma_count.get(sem, 0)
        for b in bufs:
            if ("D", sem) in b.writers:
                b.writers[("D", sem)] = v

    def barrier(self):
        self.nbar += 1
        for e in self.ENGS:
            self.ops[e].append({"bar": True})

    BOOT = 1 << 20

    def emit(self, block, sems, dma_sems, bar_sem, boot_sem, clear_range,
             pad_aps=None, nc=None):
        val_of = {}
        for e in self.ENGS:
            acc, m = 0, {}
            for i, op in enumerate(self.ops[e]):
                if not op.get("bar") and op.get("inc"):
                    acc += 1
                m[i] = acc
            val_of[e] = m
        flush = {}
        for e in ("q", "g"):
            c, totals = {}, []
            for op in self.ops[e]:
                if op.get("bar"):
                    totals.append(dict(c))
                elif op["dma_sem"] is not None:
                    c[op["dma_sem"]] = c.get(op["dma_sem"], 0) + 1
            totals.append(dict(c))
            flush[e] = totals

        def emit_stream(ek, ebass):
            waited, bar_i = {}, 0
            is_q = ek in ("q", "g")
            cur_ph = [None]
            scope = [None]

            def set_ph(p):
                if p == cur_ph[0] or nc is None:
                    return
                if scope[0] is not None:
                    scope[0].__exit__(None, None, None)
                cur_ph[0] = p
                scope[0] = nc.named_scope(p) if p else None
                if scope[0] is not None:
                    scope[0].__enter__()
            import os as _os
            if _os.environ.get("KDBG_NOBOOT") or _os.environ.get("KDBG_NOSTART"):
                pass
            elif ek == "g":
                if not _os.environ.get("KDBG_NODMARESET"):
                    ebass.dma_reset(clear_range)
                ebass.sem_clear(clear_range)
                ebass.sem_inc(boot_sem, self.BOOT)
            else:
                ebass.wait_ge(boot_sem, self.BOOT)
            for op in self.ops[ek]:
                if op.get("bar"):
                    set_ph(None)
                    if is_q:
                        for sm, c in flush[ek][bar_i].items():
                            h = dma_sems[sm]
                            if waited.get(h, -1) < 16 * c:
                                ebass.wait_ge(h, 16 * c)
                                waited[h] = 16 * c
                    bar_i += 1
                    ebass.nop().then_inc(bar_sem, 1)
                    ebass.wait_ge(bar_sem, 5 * bar_i)
                    continue
                for key, v in sorted(op["waits"].items(), key=str):
                    if key[0] == "E":
                        h, val = sems[key[1]], val_of[key[1]][v]
                        assert val >= 1
                    else:
                        h, val = dma_sems[key[1]], v
                    if waited.get(h, -1) < val:
                        ebass.wait_ge(h, val)
                        waited[h] = val
                set_ph(op.get("ph"))
                if op.get("pad") and pad_aps and ek in pad_aps:
                    a, b = pad_aps[ek]
                    if ek == "v":
                        ebass.tensor_copy(a, b)
                    else:
                        ebass.copy(a, b)
                ins = op["fn"](ebass)
                if op["dma_sem"] is not None:
                    ins.then_inc(dma_sems[op["dma_sem"]], 16)
                elif op.get("inc"):
                    ins.then_inc(sems[ek], 1)
            set_ph(None)
            if ek == "g" and not (_os.environ.get("KDBG_NOBOOT")
                                  or _os.environ.get("KDBG_NOEND")):
                ebass.sem_clear(range(boot_sem.num, boot_sem.num + 1))

        block.tensor(lambda e: emit_stream("t", e))
        block.vector(lambda e: emit_stream("v", e))
        block.scalar(lambda e: emit_stream("s", e))
        block.sync(lambda e: emit_stream("q", e))
        block.gpsimd(lambda e: emit_stream("g", e))


def build_program(debug_taps=()):
    nc = bass.Bass("TRN2", target_bir_lowering=False, debug=False)

    x_full = nc.dram_tensor("x_full", [S, D], F32, kind="ExternalInput").ap()
    x_own = nc.dram_tensor("x_own", [OWN, D], F32, kind="ExternalInput").ap()
    cmask_in = nc.dram_tensor("cmask", [4, 128, 1024], BF16,
                              kind="ExternalInput").ap()
    wqT = nc.dram_tensor("wqT", [D, D], F32, kind="ExternalInput").ap()
    wkT = nc.dram_tensor("wkT", [D, D], F32, kind="ExternalInput").ap()
    wvT = nc.dram_tensor("wvT", [D, D], F32, kind="ExternalInput").ap()
    woT = nc.dram_tensor("woT", [D, D], F32, kind="ExternalInput").ap()
    bq_in = nc.dram_tensor("bq", [D], F32, kind="ExternalInput").ap()
    bk_in = nc.dram_tensor("bk", [D], F32, kind="ExternalInput").ap()
    bv_in = nc.dram_tensor("bv", [D], F32, kind="ExternalInput").ap()
    bo_in = nc.dram_tensor("bo", [D], F32, kind="ExternalInput").ap()
    wrT = nc.dram_tensor("wrT", [D, NE], F32, kind="ExternalInput").ap()
    ebias_in = nc.dram_tensor("ebias", [NE], F32, kind="ExternalInput").ap()
    wup_in = nc.dram_tensor("wup", [NE, D, MI], F32, kind="ExternalInput").ap()
    wdn_in = nc.dram_tensor("wdn", [NE, MI, D], F32, kind="ExternalInput").ap()
    shup_in = nc.dram_tensor("shup", [D, SI], F32, kind="ExternalInput").ap()
    shdn_in = nc.dram_tensor("shdn", [SI, D], F32, kind="ExternalInput").ap()
    y_out = nc.dram_tensor("y", [OWN, D], F32, kind="ExternalOutput").ap()

    k_hbm = nc.dram_tensor("k_hbm", [H, 128, S], BF16).ap()
    v_hbm = nc.dram_tensor("v_hbm", [H, S // 128, 128, 128], BF16).ap()
    xn_own_hbm = nc.dram_tensor("xn_own_hbm", [NDC, 128, OWN], BF16).ap()
    q_hbm = nc.dram_tensor("q_hbm", [H, 128, OWN], BF16).ap()
    ctx_hbm = nc.dram_tensor("ctx_hbm", [H, 128, OWN], BF16).ap()
    a_hbm = nc.dram_tensor("a_hbm", [NE, MI // 128, 128, OWN], BF16).ap()
    as_hbm = nc.dram_tensor("as_hbm", [SI // 128, 128, OWN], BF16).ap()
    xnf_dbg = (nc.dram_tensor("xnf_dbg", [NDC, 128, S], BF16).ap()
               if "xnf_dbg" in debug_taps else None)

    ident = nc.inline_tensor(np.eye(128, dtype=ml_dtypes.bfloat16),
                             name="ident").ap()
    e_ = np.arange(16)
    lt16_np = (e_[None, :] < e_[:, None]).astype(np.float32).reshape(1, 256)
    g_ = np.arange(4)
    lt4_np = (g_[None, :] < g_[:, None]).astype(np.float32).reshape(1, 16)
    lt16_c = nc.inline_tensor(lt16_np, name="lt16").ap()
    lt4_c = nc.inline_tensor(lt4_np, name="lt4").ap()

    dep = Dep()
    taps = {"k_hbm": k_hbm, "v_hbm": v_hbm, "q_hbm": q_hbm,
            "xn_own_hbm": xn_own_hbm, "ctx_hbm": ctx_hbm, "a_hbm": a_hbm,
            "as_hbm": as_hbm, "xnf_dbg": xnf_dbg}
    tapbuf = {"k_hbm": "k", "v_hbm": "v", "q_hbm": "q", "xn_own_hbm": "xn",
              "ctx_hbm": "ctx", "a_hbm": "a", "as_hbm": "as",
              "xnf_dbg": "xnf"}
    tap_outs = {}
    for name in debug_taps:
        src = taps[name]
        tap_outs[name] = nc.dram_tensor(
            "tap_" + name, list(src.tensor.shape), BF16,
            kind="ExternalOutput").ap()

    import contextlib
    with contextlib.ExitStack() as ctx_mgr:
        def alloc(name, shape, dtype):
            return ctx_mgr.enter_context(nc.sbuf_tensor(name, shape, dtype))

        A = alloc("A", [128, 32768], BF16)
        Ba = alloc("Ba", [128, 32768], BF16)
        C = alloc("C", [128, 4096], F32)
        Dn = alloc("Dn", [128, 16384], BF16)
        E = alloc("E", [128, 2048], F32)
        idsb = alloc("idsb", [128, 128], BF16)
        pTs = alloc("pTs", [128, 2, 512], BF16)
        kst = alloc("kst", [128, 2, 512], BF16)
        trst = alloc("trst", [128, 4096], BF16)
        f2 = alloc("f2", [128, 2048], F32)
        ln1 = alloc("ln1", [128, 16], F32)
        bqkb = alloc("bqkb", [128, 2, 16], F32)
        ebias_bc = alloc("ebias_bc", [128, 16], F32)
        comb = alloc("comb", [128, 256], F32)
        rsc = alloc("rsc", [128, 32], F32)
        lt16_sb = alloc("lt16_sb", [128, 256], F32)
        lt4_sb = alloc("lt4_sb", [128, 16], F32)
        PS = ctx_mgr.enter_context(nc.psum_tensor("PS", [128, 4096], F32))
        assert nc.sbuf_bytes_remaining >= 0

        def bank(b, n=512):
            return PS[:, b * 512:b * 512 + n]

        def bank_bf(b):
            return PS[:, b * 512:(b + 1) * 512].bitcast(BF16)

        bar_sem = ctx_mgr.enter_context(nc.semaphore("sem_bar"))
        boot_sem = ctx_mgr.enter_context(nc.semaphore("sem_boot"))
        sems = {e: ctx_mgr.enter_context(nc.semaphore("sem_" + e))
                for e in Dep.ENGS}
        dma_sems = {}

        def dsem(name):
            if name not in dma_sems:
                dma_sems[name] = ctx_mgr.enter_context(nc.semaphore("d_" + name))
            return name

        bufs = {}

        def buf(name):
            if name not in bufs:
                bufs[name] = Buf(name)
            return bufs[name]

        PSB = [buf(f"ps{i}") for i in range(8)]
        HB = {n: buf("hbm_" + n) for n in
              ("k", "v", "q", "xn", "ctx", "a", "as", "xnf")}
        bA, bB, bE = buf("A"), buf("B"), buf("E")

        for nm in ("cst",):
            dsem(nm)
        dep.dma("q", "cst", lambda e: e.dma_start(out=idsb[:, :], in_=ident[:, :]),
                writes=[buf("idsb")])
        bq_r = bq_in.rearrange("(h p) -> p h", p=128)
        bk_r = bk_in.rearrange("(h p) -> p h", p=128)
        dep.dma("q", "cst", lambda e: e.dma_start(out=bqkb[:, 0, :], in_=bq_r,
                                          allow_slow_non_contiguous=True),
                writes=[buf("bqkb")])
        dep.dma("q", "cst", lambda e: e.dma_start(out=bqkb[:, 1, :], in_=bk_r,
                                          allow_slow_non_contiguous=True),
                writes=[buf("bqkb")])
        dep.dma("q", "cst", lambda e: e.dma_start(
            out=ebias_bc[:, :], in_=ebias_in[None, :].broadcast_to([128, NE])),
            writes=[buf("ebias_bc")])
        dep.dma("q", "cst", lambda e: e.dma_start(
            out=E[:, :], in_=bv_in[None, :].broadcast_to([128, D])),
            writes=[bE])
        dep.dma("q", "cst", lambda e: e.dma_start(
            out=lt16_sb[:, :], in_=lt16_c[:, :].broadcast_to([128, 256])),
            writes=[buf("lt16")])
        dep.dma("q", "cst", lambda e: e.dma_start(
            out=lt4_sb[:, :], in_=lt4_c[:, :].broadcast_to([128, 16])),
            writes=[buf("lt4")])
        dep.seal("cst", [buf("idsb"), buf("bqkb"), buf("ebias_bc"), bE,
                         buf("lt16"), buf("lt4")])
        dep.barrier()
        dep.v(lambda e: e.tensor_copy(ln1[:, 15:16], idsb[:, 0:1]),
              reads=[buf("idsb")], writes=[buf("padseed")])
        dep.s(lambda e: e.copy(ln1[:, 13:14], idsb[:, 1:2]),
              reads=[buf("idsb")], writes=[buf("padseed")])

        xt_v = C[:, :].rearrange("p (sl d) -> p sl d", sl=2)
        bC = [buf("c0"), buf("c1")]
        xtok = trst[:, 0:2048]
        bXT = buf("xtok")
        bLN = buf("ln1")

        def emit_ln(cs):
            xt = xt_v[:, cs]
            dep.v(lambda e, xt=xt: e.reduce_sum(ln1[:, 0:1], xt, axis=AX.X),
                  reads=[bC[cs]], writes=[bLN])
            dep.s(lambda e: e.mul(ln1[:, 1:2], ln1[:, 0:1], -1.0 / D),
                  reads=[bLN], writes=[bLN])
            dep.v(lambda e, xt=xt: e.tensor_mul(
                PS[:, 3072:4096], xt[:, 0:1024], xt[:, 0:1024]),
                reads=[bC[cs]], writes=[PSB[6], PSB[7]])
            dep.v(lambda e: e.reduce_sum(ln1[:, 2:3], PS[:, 3072:4096],
                                         axis=AX.X),
                  reads=[PSB[6], PSB[7]], writes=[bLN])
            dep.v(lambda e, xt=xt: e.tensor_mul(
                PS[:, 3072:4096], xt[:, 1024:2048], xt[:, 1024:2048]),
                reads=[bC[cs]], writes=[PSB[6], PSB[7]])
            dep.v(lambda e: e.reduce_sum(ln1[:, 3:4], PS[:, 3072:4096],
                                         axis=AX.X),
                  reads=[PSB[6], PSB[7]], writes=[bLN])
            dep.v(lambda e: e.tensor_add(ln1[:, 4:5], ln1[:, 2:3],
                                         ln1[:, 3:4]),
                  reads=[bLN], writes=[bLN])
            dep.v(lambda e: e.tensor_mul(ln1[:, 5:6], ln1[:, 1:2], ln1[:, 1:2]),
                  reads=[bLN], writes=[bLN])
            dep.v(lambda e: e.tensor_scalar(ln1[:, 6:7], ln1[:, 4:5], 1.0 / D,
                                            ln1[:, 5:6], ALU.mult, ALU.subtract),
                  reads=[bLN], writes=[bLN])
            dep.v(lambda e: e.tensor_single_scalar(ln1[:, 7:8], ln1[:, 6:7],
                                               LN_EPS, ALU.add),
                  reads=[bLN], writes=[bLN])
            dep.v(lambda e: e.reciprocal(ln1[:, 8:9], ln1[:, 7:8]),
                  reads=[bLN], writes=[bLN])
            dep.s(lambda e: e.activation(ln1[:, 9:10], ln1[:, 8:9], AF.Sqrt),
                  reads=[bLN], writes=[bLN])
            dep.v(lambda e: e.tensor_mul(ln1[:, 10:11], ln1[:, 1:2], ln1[:, 9:10]),
                  reads=[bLN], writes=[bLN])
            dep.s(lambda e, xt=xt: e.activation(
                xtok[:, :], xt, AF.Identity,
                bias=ln1[:, 10:11], scale=ln1[:, 9:10]),
                reads=[bC[cs], bLN], writes=[bXT])

        def emit_transpose16(dst_view, dst_buf):
            for half8 in range(2):
                pb = 4 + half8
                for j in range(8):
                    dc = half8 * 8 + j
                    dep.t(lambda e, dc=dc, pb=pb, j=j: e.transpose(
                        bank_bf(pb)[:, j * 128:(j + 1) * 128],
                        xtok[:, dc * 128:(dc + 1) * 128], idsb[:, :]),
                        reads=[bXT, buf("idsb")], writes=[PSB[pb]])
                dep.v(lambda e, pb=pb, half8=half8: e.tensor_copy(
                    dst_view[:, half8 * 8:(half8 + 1) * 8, :],
                    bank_bf(pb).rearrange("p (a b) -> p a b", a=8)),
                    reads=[PSB[pb]], writes=[dst_buf])

        dep.phase("P0")
        wk_bf = A[:, :].rearrange("p (dc e) -> p dc e", dc=NDC)
        wv_bf = Ba[:, :].rearrange("p (dc e) -> p dc e", dc=NDC)
        dsem("wk"); dsem("wv"); dsem("xld0"); dsem("xld1"); dsem("xfd")
        dsem("kw0"); dsem("kw1"); dsem("vw0"); dsem("vw1")
        wkT_r = wkT.rearrange("(dc p) e -> p dc e", p=128)
        wvT_r = wvT.rearrange("(dc p) e -> p dc e", p=128)
        dep.dma("g", "wk", lambda e: e.dma_start(out=wk_bf[:, :, :], in_=wkT_r),
                writes=[bA])
        dep.dma("g", "wv", lambda e: e.dma_start(out=wv_bf[:, :, :], in_=wvT_r),
                writes=[bB])

        xn_ch = Dn[:, :].rearrange("p (sl dc t) -> p sl dc t", sl=2, dc=NDC)
        bXN = [buf("xn0"), buf("xn1")]
        bKst = [buf("kst0"), buf("kst1")]

        for cc in range(S // 512):
            sl = cc % 2
            ch = xn_ch[:, sl]
            bx = bXN[sl]
            for tl in range(4):
                t0 = cc * 512 + tl * 128
                cs = tl % 2
                dep.dma("q", "xld%d" % cs, lambda e, s=x_full[t0:t0 + 128, :],
                        d=xt_v[:, cs]: e.dma_start(out=d, in_=s),
                        writes=[bC[cs]])
                emit_ln(cs)
                emit_transpose16(ch[:, :, tl * 128:(tl + 1) * 128], bx)
            if xnf_dbg is not None:
                dstd = xnf_dbg[:, :, cc * 512:(cc + 1) * 512].rearrange(
                    "dc p t -> p dc t")
                dep.dma("q", "xfd", lambda e, d=dstd, s=ch:
                        e.dma_start(out=d, in_=s),
                        reads=[bx], writes=[HB["xnf"]])
            for ec in range(H):
                pb = ec % 2
                for dc in range(NDC):
                    dep.t(lambda e, dc=dc, ec=ec, pb=pb, ch=ch: e.matmul(
                        bank(pb), wk_bf[:, dc, ec * 128:(ec + 1) * 128],
                        ch[:, dc, :], start=(dc == 0), stop=(dc == NDC - 1)),
                        reads=[bA, bx], writes=[PSB[pb]])
                ks = kst[:, pb, :]
                dep.s(lambda e, pb=pb, ks=ks, ec=ec: e.activation(
                    ks, bank(pb), AF.Identity, bias=bqkb[:, 1, ec:ec + 1]),
                    reads=[PSB[pb], buf("bqkb")], writes=[bKst[pb]])
                dep.dma("q", "kw%d" % pb, lambda e,
                        d=k_hbm[ec, :, cc * 512:(cc + 1) * 512], s=ks:
                        e.dma_start(out=d, in_=s),
                        reads=[bKst[pb]], writes=[HB["k"]])
            for tl in range(4):
                for eq in range(4):
                    pb = 2 + (eq % 2)
                    for dc in range(NDC):
                        dep.t(lambda e, dc=dc, tl=tl, eq=eq, pb=pb, ch=ch:
                              e.matmul(
                                  bank(pb), ch[:, dc, tl * 128:(tl + 1) * 128],
                                  wv_bf[:, dc, eq * 512:(eq + 1) * 512],
                                  start=(dc == 0), stop=(dc == NDC - 1)),
                              reads=[bB, bx], writes=[PSB[pb]])
                    vs = kst[:, pb - 2, :]
                    dep.v(lambda e, pb=pb, vs=vs, eq=eq: e.tensor_tensor(
                        vs, bank(pb), E[:, eq * 512:(eq + 1) * 512], ALU.add),
                        reads=[PSB[pb], bE], writes=[bKst[pb - 2]])
                    dst = v_hbm[eq * 4:(eq + 1) * 4, cc * 4 + tl, :, :].rearrange(
                        "h tp dk -> tp h dk")
                    dep.dma("q", "vw%d" % (pb - 2), lambda e, d=dst, s=vs:
                            e.dma_start(out=d, in_=s),
                            reads=[bKst[pb - 2]], writes=[HB["v"]])
        dep.barrier()

        dep.phase("PQ")
        wq_bf = A[:, :].rearrange("p (dc e) -> p dc e", dc=NDC)
        xn_ownB = Ba[:, :].rearrange("p (dc t) -> p dc t", dc=NDC)
        dsem("wq"); dsem("xnw"); dsem("qw0"); dsem("qw1")
        wqT_r = wqT.rearrange("(dc p) e -> p dc e", p=128)
        dep.dma("g", "wq", lambda e: e.dma_start(out=wq_bf[:, :, :], in_=wqT_r),
                writes=[bA])
        for ot in range(16):
            cs = ot % 2
            dep.dma("q", "xld%d" % cs, lambda e,
                    s=x_own[ot * 128:(ot + 1) * 128, :],
                    d=xt_v[:, cs]: e.dma_start(out=d, in_=s), writes=[bC[cs]])
            emit_ln(cs)
            emit_transpose16(xn_ownB[:, :, ot * 128:(ot + 1) * 128], bB)
            if ot % 4 == 3:
                oc = ot // 4
                dst = xn_own_hbm[:, :, oc * 512:(oc + 1) * 512].rearrange(
                    "dc p t -> p dc t")
                dep.dma("q", "xnw", lambda e, d=dst,
                        s=xn_ownB[:, :, oc * 512:(oc + 1) * 512]:
                        e.dma_start(out=d, in_=s),
                        reads=[bB], writes=[HB["xn"]])
        for ec in range(H):
            for oc in range(4):
                pb = oc % 2
                for dc in range(NDC):
                    dep.t(lambda e, dc=dc, ec=ec, oc=oc, pb=pb: e.matmul(
                        bank(pb), wq_bf[:, dc, ec * 128:(ec + 1) * 128],
                        xn_ownB[:, dc, oc * 512:(oc + 1) * 512],
                        start=(dc == 0), stop=(dc == NDC - 1)),
                        reads=[bA, bB], writes=[PSB[pb]])
                qs = kst[:, pb, :]
                dep.s(lambda e, pb=pb, qs=qs, ec=ec: e.activation(
                    qs, bank(pb), AF.Identity, bias=bqkb[:, 0, ec:ec + 1]),
                    reads=[PSB[pb], buf("bqkb")], writes=[bKst[pb]])
                dep.dma("q", "qw%d" % pb, lambda e,
                        d=q_hbm[ec, :, oc * 512:(oc + 1) * 512], s=qs:
                        e.dma_start(out=d, in_=s),
                        reads=[bKst[pb]], writes=[HB["q"]])
        dep.barrier()

        dep.phase("PA")
        Kv = A[:, :].rearrange("p (sl x) -> p sl x", sl=4)
        ctxst = f2[:, 0:512].bitcast(BF16).rearrange("p (sl x) -> p sl x", sl=2)
        Ksl = [Kv[:, 0, 0:4096], Kv[:, 0, 4096:8192]]
        Vsl = [Kv[:, 1, 0:4096].rearrange("p (kc dk) -> p kc dk", kc=32),
               Kv[:, 1, 4096:8192].rearrange("p (kc dk) -> p kc dk", kc=32)]
        Qsl = [Kv[:, 2, 0:2048], Kv[:, 2, 2048:4096]]
        probs = [Ba[:, i * 4096:(i + 1) * 4096] for i in range(4)]
        scors = [[Dn[:, i * 4096:(i + 1) * 4096] for i in range(4)],
                 [A[:, 20480:24576], A[:, 24576:28672], A[:, 28672:32768],
                  Ba[:, 20480:24576]]]
        maskb = Ba[:, 24576:28672].rearrange("p (qt w) -> p qt w", qt=4)
        bb = {n: buf("pa_" + n) for n in
              ("k0", "k1", "v0", "v1", "q0", "q1", "p0", "p1", "p2", "p3",
               "s00", "s10", "s20", "s30", "s01", "s11", "s21", "s31",
               "mask", "rsc0", "rsc1", "pT0", "pT1", "cx0", "cx1")}
        dsem("kr0"); dsem("kr1"); dsem("vr0"); dsem("vr1")
        dsem("qr0"); dsem("qr1"); dsem("cxw0"); dsem("cxw1"); dsem("msk")
        dsem("pT0"); dsem("pT1")
        dep.dma("q", "msk", lambda e: e.dma_start(
            out=maskb[:, :, :], in_=cmask_in.rearrange("qt p w -> p qt w")),
            writes=[bb["mask"]])

        def load_head(h):
            sl = h % 2
            dep.dma("q", "kr%d" % sl, lambda e, s=k_hbm[h], d=Ksl[sl]:
                    e.dma_start(out=d, in_=s),
                    reads=[HB["k"]], writes=[bb["k%d" % sl]])
            src = v_hbm[h].rearrange("kc tp dk -> tp kc dk")
            dep.dma("q", "vr%d" % sl, lambda e, s=src, d=Vsl[sl]:
                    e.dma_start(out=d, in_=s),
                    reads=[HB["v"]], writes=[bb["v%d" % sl]])
            dep.dma("q", "qr%d" % sl, lambda e, s=q_hbm[h], d=Qsl[sl]:
                    e.dma_start(out=d, in_=s),
                    reads=[HB["q"]], writes=[bb["q%d" % sl]])

        def pa_scores(h, g):
            sl = h % 2
            par = g % 2
            bk_, bq_ = bb["k%d" % sl], bb["q%d" % sl]
            K_, Q_ = Ksl[sl], Qsl[sl]
            kg = 1024 * (g + 1)
            nkc = kg // 512
            sb_banks = (0, 1, 5, 6)
            for qt in range(4):
                own0 = g * 512 + qt * 128
                for kc in range(nkc):
                    pb = sb_banks[(qt * nkc + kc) % 4]
                    dep.t(lambda e, kc=kc, pb=pb, own0=own0, Q_=Q_, K_=K_:
                          e.matmul(
                              bank(pb), Q_[:, own0:own0 + 128],
                              K_[:, kc * 512:(kc + 1) * 512],
                              start=True, stop=True),
                          reads=[bq_, bk_], writes=[PSB[pb]])
                    if kc >= nkc - 2:
                        w0 = (kc - (nkc - 2)) * 512
                        dep.v(lambda e, kc=kc, pb=pb, qt=qt, w0=w0, par=par:
                              e.tensor_tensor(
                                  scors[par][qt][:, kc * 512:(kc + 1) * 512],
                                  bank(pb), maskb[:, qt, w0:w0 + 512],
                                  ALU.add),
                              reads=[PSB[pb], bb["mask"]],
                              writes=[bb["s%d%d" % (qt, par)]])
                    else:
                        dep.v(lambda e, kc=kc, pb=pb, qt=qt, par=par:
                              e.tensor_copy(
                                  scors[par][qt][:, kc * 512:(kc + 1) * 512],
                                  bank(pb)),
                              reads=[PSB[pb]], writes=[bb["s%d%d" % (qt, par)]])

        def pa_exps(h, g):
            kg = 1024 * (g + 1)
            par = g % 2
            for qt in range(4):
                dep.s(lambda e, kg=kg, qt=qt, par=par: e.activation(
                    scors[par][qt][:, 0:kg], scors[par][qt][:, 0:kg], AF.Exp,
                    bias=0.0, scale=INV_SQRT_DK),
                    reads=[bb["s%d%d" % (qt, par)]],
                    writes=[bb["s%d%d" % (qt, par)]])

        def pa_sums(h, g):
            kg = 1024 * (g + 1)
            par = g % 2

            def rsum(qt):
                dep.v(lambda e, kg=kg, qt=qt, par=par: e.reduce_sum(
                    rsc[:, 16*par+8+qt:16*par+9+qt],
                    scors[par][qt][:, 0:kg], axis=AX.X),
                    reads=[bb["s%d%d" % (qt, par)]],
                    writes=[bb["rsc%d" % par]])

            def recip(qt):
                dep.v(lambda e, qt=qt, par=par: e.reciprocal(
                    rsc[:, 16*par+12+qt:16*par+13+qt],
                    rsc[:, 16*par+8+qt:16*par+9+qt]),
                    reads=[bb["rsc%d" % par]], writes=[bb["rsc%d" % par]])

            def div(qt):
                dep.v(lambda e, kg=kg, qt=qt, par=par: e.tensor_scalar(
                    scors[par][qt][:, 0:kg], scors[par][qt][:, 0:kg],
                    rsc[:, 16*par+12+qt:16*par+13+qt], None, ALU.mult),
                    reads=[bb["rsc%d" % par], bb["s%d%d" % (qt, par)]],
                    writes=[bb["s%d%d" % (qt, par)]])

            rsum(0); rsum(1); recip(0); rsum(2); recip(1); div(0)
            rsum(3); recip(2); div(1); recip(3); div(2); div(3)

        def pa_trav(h, g):
            sl = h % 2
            par = g % 2
            bv_ = bb["v%d" % sl]
            V_ = Vsl[sl]
            kg = 1024 * (g + 1)
            nk = kg // 128

            def emit_trdma(j):
                psl = j % 2
                pb = 2 + psl
                for qt in range(4):
                    dep.t(lambda e, j=j, qt=qt, pb=pb, par=par: e.transpose(
                        bank_bf(pb)[:, qt * 128:(qt + 1) * 128],
                        scors[par][qt][:, j * 128:(j + 1) * 128], idsb[:, :]),
                        reads=[bb["s%d%d" % (qt, par)], buf("idsb")],
                        writes=[PSB[pb]])
                dep.s(lambda e, pb=pb, psl=psl: e.copy(
                    pTs[:, psl, :], bank_bf(pb)[:, 0:512]),
                    reads=[PSB[pb]], writes=[bb["pT%d" % psl]])

            def emit_av(j):
                dep.t(lambda e, j=j, nk=nk, V_=V_: e.matmul(
                    bank(4), V_[:, j, :], pTs[:, j % 2, :],
                    start=(j == 0), stop=(j == nk - 1)),
                    reads=[bv_, bb["pT%d" % (j % 2)]], writes=[PSB[4]])

            emit_trdma(0)
            for j in range(1, nk):
                emit_trdma(j)
                emit_av(j - 1)
            emit_av(nk - 1)

        def pa_ctx(h, g):
            csl = g % 2
            cst_ = ctxst[:, csl, :]
            dep.v(lambda e, cst_=cst_: e.tensor_copy(cst_, bank(4)),
                  reads=[PSB[4]], writes=[bb["cx%d" % csl]])
            dep.dma("q", "cxw%d" % csl, lambda e, h=h, g=g, cst_=cst_:
                    e.dma_start(
                        out=ctx_hbm[h, :, g * 512:(g + 1) * 512], in_=cst_),
                    reads=[bb["cx%d" % csl]], writes=[HB["ctx"]])

        load_head(0)
        pa_scores(0, 0)
        seq = [(h, g) for h in range(H) for g in range(4)]
        for i, (h, g) in enumerate(seq):
            pa_exps(h, g)
            pa_sums(h, g)
            if i > 0:
                pa_trav(*seq[i - 1])
            if g == 0 and h + 1 < H:
                load_head(h + 1)
            if i + 1 < len(seq):
                pa_scores(*seq[i + 1])
            if i > 0:
                pa_ctx(*seq[i - 1])
        pa_trav(*seq[-1])
        pa_ctx(*seq[-1])
        dep.barrier()

        dep.phase("PC1")
        shup_lo = A[:, :].rearrange("p (dc i) -> p dc i", dc=8)
        shup_hi = Ba[:, :].rearrange("p (dc i) -> p dc i", dc=8)

        def shup_v(dc):
            return shup_lo[:, dc] if dc < 8 else shup_hi[:, dc - 8]

        dsem("shu"); dsem("asw"); dsem("xnr0"); dsem("xnr1")
        shup_r = shup_in.rearrange("(dc p) i -> p dc i", p=128)
        dep.dma("g", "shu", lambda e: e.dma_start(
            out=shup_lo[:, :, :], in_=shup_r[:, 0:8]), writes=[bA])
        dep.dma("g", "shu", lambda e: e.dma_start(
            out=shup_hi[:, :, :], in_=shup_r[:, 8:16]), writes=[bB])
        xn_ch2 = Dn[:, :].rearrange("p (sl dc t) -> p sl dc t", sl=2, dc=NDC)
        asT = trst[:, :].rearrange("p (ic t) -> p ic t", ic=32)
        bAS = buf("asT")
        a_q = f2[:, 0:512].bitcast(BF16).rearrange("p (sl x) -> p sl x", sl=2)
        rtmp = f2[:, 1024:2048].rearrange("p (sl x) -> p sl x", sl=2)
        bAQ = [buf("aq0"), buf("aq1")]
        bRT = [buf("rt0"), buf("rt1")]

        for oc in range(4):
            sl = oc % 2
            ch = xn_ch2[:, sl]
            bx = bXN[sl]
            src = xn_own_hbm[:, :, oc * 512:(oc + 1) * 512].rearrange(
                "dc p t -> p dc t")
            dep.dma("q", "xnr%d" % sl, lambda e, s=src, d=ch:
                    e.dma_start(out=d, in_=s),
                    reads=[HB["xn"]], writes=[bx])
            for k in range(4):
                ts = oc * 4 + k
                for ih in range(8):
                    pb = ih % 2
                    for dc in range(NDC):
                        dep.t(lambda e, dc=dc, k=k, ih=ih, pb=pb, ch=ch:
                              e.matmul(
                                  bank(pb), ch[:, dc, k * 128:(k + 1) * 128],
                                  shup_v(dc)[:, ih * 512:(ih + 1) * 512],
                                  start=(dc == 0), stop=(dc == NDC - 1)),
                              reads=[bA, bB, bx], writes=[PSB[pb]])
                    rt = rtmp[:, pb, :]
                    dep.s(lambda e, pb=pb, rt=rt: e.activation(
                        rt, bank(pb), AF.Relu),
                        reads=[PSB[pb]], writes=[bRT[pb]])
                    aq = a_q[:, pb, :]
                    dep.v(lambda e, pb=pb, rt=rt, aq=aq: e.tensor_tensor(
                        aq, bank(pb), rt, ALU.mult),
                        reads=[PSB[pb], bRT[pb]], writes=[bAQ[pb]])
                    tb = 2 + pb
                    for jj in range(4):
                        dep.t(lambda e, jj=jj, tb=tb, aq=aq: e.transpose(
                            bank_bf(tb)[:, jj * 128:(jj + 1) * 128],
                            aq[:, jj * 128:(jj + 1) * 128], idsb[:, :]),
                            reads=[bAQ[tb - 2], buf("idsb")], writes=[PSB[tb]])
                    dep.v(lambda e, tb=tb, ih=ih: e.tensor_copy(
                        asT[:, ih * 4:(ih + 1) * 4, :].rearrange(
                            "p a b -> p (a b)"),
                        bank_bf(tb)[:, 0:512]),
                        reads=[PSB[tb]], writes=[bAS])
                dst = as_hbm[:, :, ts * 128:(ts + 1) * 128].rearrange(
                    "ic p t -> p ic t")
                dep.dma("q", "asw", lambda e, d=dst: e.dma_start(
                    out=d, in_=asT[:, :, :]),
                    reads=[bAS], writes=[HB["as"]])
        dep.barrier()

        dep.phase("PR")
        xn_own = A[:, :].rearrange("p (dc t) -> p dc t", dc=NDC)
        for oc in range(4):
            src = xn_own_hbm[:, :, oc * 512:(oc + 1) * 512].rearrange(
                "dc p t -> p dc t")
            dst = xn_own[:, :, oc * 512:(oc + 1) * 512]
            dep.dma("q", "xnrA", lambda e, s=src, d=dst: e.dma_start(out=d, in_=s),
                    reads=[HB["xn"]], writes=[bA])
        wr_bf = Ba[:, 0:256].rearrange("p (dc e) -> p dc e", dc=NDC)
        wr_src = wrT.rearrange("(dc p) e -> p dc e", p=128)
        dsem("wr"); dsem("xnrA")
        dep.dma("g", "wr", lambda e: e.dma_start(out=wr_bf[:, :, :], in_=wr_src),
                writes=[bB])
        fR = f2
        bF2 = buf("f2")
        bCMB = buf("comb")
        for ts in range(16):
            pb = ts % 2
            for dc in range(NDC):
                dep.t(lambda e, dc=dc, ts=ts, pb=pb: e.matmul(
                    bank(pb, 16), xn_own[:, dc, ts * 128:(ts + 1) * 128],
                    wr_bf[:, dc, :], start=(dc == 0), stop=(dc == NDC - 1)),
                    reads=[bA, bB], writes=[PSB[pb]])
            lg = fR[:, 0:16]
            dep.s(lambda e, pb=pb, lg=lg: e.activation(
                lg, bank(pb, 16), AF.Sigmoid),
                reads=[PSB[pb]], writes=[bF2])
            sg = fR[:, 16:32]
            dep.v(lambda e, lg=lg, sg=sg: e.tensor_add(sg, lg, ebias_bc[:, :]),
                  reads=[bF2, buf("ebias_bc")], writes=[bF2])
            t64 = fR[:, 32:96].rearrange("p (g a b) -> p g a b", g=4, a=4)
            t64b = fR[:, 96:160].rearrange("p (g a b) -> p g a b", g=4, a=4)
            sg_g = sg.rearrange("p (g a) -> p g a", g=4)
            m_self = sg_g.unsqueeze(3).broadcast_to([128, 4, 4, 4])
            m_comp = sg_g.unsqueeze(2).broadcast_to([128, 4, 4, 4])
            dep.v(lambda e, t64=t64, m_comp=m_comp, m_self=m_self:
                  e.tensor_tensor(t64, m_comp, m_self, ALU.is_gt),
                  reads=[bF2], writes=[bF2])
            dep.v(lambda e, t64b=t64b, m_comp=m_comp, m_self=m_self:
                  e.tensor_tensor(t64b, m_comp, m_self, ALU.is_equal),
                  reads=[bF2], writes=[bF2])
            lt4g = lt4_sb[:, :].rearrange("p (a b) -> p a b", a=4) \
                .unsqueeze(1).broadcast_to([128, 4, 4, 4])
            dep.v(lambda e, t64b=t64b, lt4g=lt4g:
                  e.tensor_tensor(t64b, t64b, lt4g, ALU.mult),
                  reads=[bF2, buf("lt4")], writes=[bF2])
            dep.v(lambda e, t64=t64, t64b=t64b:
                  e.tensor_tensor(t64, t64, t64b, ALU.add),
                  reads=[bF2], writes=[bF2])
            rk = fR[:, 160:176]
            dep.v(lambda e, rk=rk, t64=t64: e.reduce_sum(
                rk.rearrange("p (g a) -> p g a", g=4), t64, axis=AX.X),
                reads=[bF2], writes=[bF2])
            sel4 = fR[:, 176:192]
            dep.v(lambda e, sel4=sel4, rk=rk: e.tensor_single_scalar(
                sel4, rk, 1.5, ALU.is_lt),
                reads=[bF2], writes=[bF2])
            dep.v(lambda e, sg=sg, sel4=sel4: e.tensor_tensor(
                fR[:, 208:224], sg, sel4, ALU.mult),
                reads=[bF2], writes=[bF2])
            gsc = fR[:, 192:196]
            dep.v(lambda e, gsc=gsc: e.reduce_sum(
                gsc.rearrange("p (g o) -> p g o", g=4),
                fR[:, 208:224].rearrange("p (g a) -> p g a", g=4), axis=AX.X),
                reads=[bF2], writes=[bF2])
            g16 = fR[:, 32:48].rearrange("p (a b) -> p a b", a=4)
            g16b = fR[:, 48:64].rearrange("p (a b) -> p a b", a=4)
            gs_self = gsc.unsqueeze(2).broadcast_to([128, 4, 4])
            gs_comp = gsc.unsqueeze(1).broadcast_to([128, 4, 4])
            dep.v(lambda e, g16=g16, gs_comp=gs_comp, gs_self=gs_self:
                  e.tensor_tensor(g16, gs_comp, gs_self, ALU.is_gt),
                  reads=[bF2], writes=[bF2])
            dep.v(lambda e, g16b=g16b, gs_comp=gs_comp, gs_self=gs_self:
                  e.tensor_tensor(g16b, gs_comp, gs_self, ALU.is_equal),
                  reads=[bF2], writes=[bF2])
            lt4v = lt4_sb[:, :].rearrange("p (a b) -> p a b", a=4)
            dep.v(lambda e, g16b=g16b, lt4v=lt4v:
                  e.tensor_tensor(g16b, g16b, lt4v, ALU.mult),
                  reads=[bF2, buf("lt4")], writes=[bF2])
            dep.v(lambda e, g16=g16, g16b=g16b:
                  e.tensor_tensor(g16, g16, g16b, ALU.add),
                  reads=[bF2], writes=[bF2])
            rkg = fR[:, 196:200]
            dep.v(lambda e, rkg=rkg, g16=g16: e.reduce_sum(
                rkg.rearrange("p (g o) -> p g o", g=4), g16, axis=AX.X),
                reads=[bF2], writes=[bF2])
            selg = fR[:, 200:204]
            dep.v(lambda e, selg=selg, rkg=rkg: e.tensor_single_scalar(
                selg, rkg, 1.5, ALU.is_lt),
                reads=[bF2], writes=[bF2])
            smask = fR[:, 208:224].rearrange("p (g a) -> p g a", g=4)
            dep.v(lambda e, smask=smask, selg=selg: e.tensor_copy(
                smask, selg.rearrange("p (g o) -> p g o", g=4)
                .broadcast_to([128, 4, 4])),
                reads=[bF2], writes=[bF2])
            msk = fR[:, 224:240]
            dep.v(lambda e, msk=msk, sg=sg: e.tensor_tensor(
                msk, sg, fR[:, 208:224], ALU.mult),
                reads=[bF2], writes=[bF2])
            t256 = fR[:, 272:528].rearrange("p (a b) -> p a b", a=16)
            t256b = fR[:, 528:784].rearrange("p (a b) -> p a b", a=16)
            mm_self = msk.unsqueeze(2).broadcast_to([128, 16, 16])
            mm_comp = msk.unsqueeze(1).broadcast_to([128, 16, 16])
            dep.v(lambda e, t256=t256, mm_comp=mm_comp, mm_self=mm_self:
                  e.tensor_tensor(t256, mm_comp, mm_self, ALU.is_gt),
                  reads=[bF2], writes=[bF2])
            dep.v(lambda e, t256b=t256b, mm_comp=mm_comp, mm_self=mm_self:
                  e.tensor_tensor(t256b, mm_comp, mm_self, ALU.is_equal),
                  reads=[bF2], writes=[bF2])
            lt16v = lt16_sb[:, :].rearrange("p (a b) -> p a b", a=16)
            dep.v(lambda e, t256b=t256b, lt16v=lt16v:
                  e.tensor_tensor(t256b, t256b, lt16v, ALU.mult),
                  reads=[bF2, buf("lt16")], writes=[bF2])
            dep.v(lambda e, t256=t256, t256b=t256b:
                  e.tensor_tensor(t256, t256, t256b, ALU.add),
                  reads=[bF2], writes=[bF2])
            rk16 = fR[:, 160:176]
            dep.v(lambda e, rk16=rk16, t256=t256: e.reduce_sum(
                rk16.rearrange("p (a o) -> p a o", a=16), t256, axis=AX.X),
                reads=[bF2], writes=[bF2])
            sel = fR[:, 176:192]
            dep.v(lambda e, sel=sel, rk16=rk16: e.tensor_single_scalar(
                sel, rk16, 1.5, ALU.is_lt),
                reads=[bF2], writes=[bF2])
            w_ = fR[:, 240:256]
            dep.v(lambda e, w_=w_, lg=lg, sel=sel: e.tensor_tensor(
                w_, lg, sel, ALU.mult),
                reads=[bF2], writes=[bF2])
            dep.v(lambda e, w_=w_: e.reduce_sum(fR[:, 256:257], w_, axis=AX.X),
                  reads=[bF2], writes=[bF2])
            dep.v(lambda e: e.tensor_single_scalar(fR[:, 256:257], fR[:, 256:257],
                                               1e-20, ALU.add),
                  reads=[bF2], writes=[bF2])
            dep.v(lambda e: e.reciprocal(fR[:, 257:258], fR[:, 256:257]),
                  reads=[bF2], writes=[bF2])
            dep.v(lambda e, ts=ts, w_=w_: e.tensor_scalar(
                comb[:, ts * 16:(ts + 1) * 16], w_, fR[:, 257:258], None,
                ALU.mult),
                reads=[bF2], writes=[bCMB])
        dep.barrier()

        dep.phase("PD1")
        wup_v = Ba[:, :].rearrange("p (sl dc i) -> p sl dc i", sl=2, dc=NDC)
        bWU = [buf("wup0"), buf("wup1")]
        dsem("wup0"); dsem("wup1"); dsem("aw0"); dsem("aw1")
        aT_st = trst[:, 0:2048].rearrange("p (sl ic t) -> p sl ic t", sl=2, ic=8)
        bAT = [buf("aT0"), buf("aT1")]
        a_tok = f2[:, 0:512].bitcast(BF16).rearrange("p (sl x) -> p sl x", sl=2)
        rtmp2 = f2[:, 1024:2048].rearrange("p (sl x) -> p sl x", sl=2)

        def load_wup(e_, sl):
            src_ = wup_in[e_].rearrange("(dc p) i -> p dc i", p=128)
            dep.dma("g", "wup%d" % sl, lambda e, s=src_, sl=sl:
                    e.dma_start(out=wup_v[:, sl], in_=s),
                    writes=[bWU[sl]])

        load_wup(0, 0)
        for ex in range(NE):
            sl = ex % 2
            if ex + 1 < NE:
                load_wup(ex + 1, 1 - sl)
            for ts in range(16):
                for ih in range(2):
                    pb = ih
                    for dc in range(NDC):
                        dep.t(lambda e, dc=dc, ts=ts, ih=ih, pb=pb, sl=sl:
                              e.matmul(
                                  bank(pb),
                                  xn_own[:, dc, ts * 128:(ts + 1) * 128],
                                  wup_v[:, sl, dc, ih * 512:(ih + 1) * 512],
                                  start=(dc == 0), stop=(dc == NDC - 1)),
                              reads=[bA, bWU[sl]], writes=[PSB[pb]])
                for ih in range(2):
                    pb = ih
                    rt = rtmp2[:, pb, :]
                    cvec = comb[:, ts * 16 + ex:ts * 16 + ex + 1]
                    dep.s(lambda e, pb=pb, rt=rt, cvec=cvec: e.activation(
                        rt, bank(pb), AF.Relu, scale=cvec),
                        reads=[PSB[pb], bCMB], writes=[bRT[pb]])
                    atk = a_tok[:, pb, :]
                    dep.v(lambda e, pb=pb, rt=rt, atk=atk: e.tensor_tensor(
                        atk, bank(pb), rt, ALU.mult),
                        reads=[PSB[pb], bRT[pb]], writes=[bAQ[pb]])
                    tb = 2 + ih
                    for jj in range(4):
                        dep.t(lambda e, jj=jj, tb=tb, atk=atk: e.transpose(
                            bank_bf(tb)[:, jj * 128:(jj + 1) * 128],
                            atk[:, jj * 128:(jj + 1) * 128], idsb[:, :]),
                            reads=[bAQ[tb - 2], buf("idsb")], writes=[PSB[tb]])
                    tsl = ts % 2
                    dep.v(lambda e, tb=tb, ih=ih, tsl=tsl: e.tensor_copy(
                        aT_st[:, tsl, ih * 4:(ih + 1) * 4, :].rearrange(
                            "p a b -> p (a b)"),
                        bank_bf(tb)[:, 0:512]),
                        reads=[PSB[tb]], writes=[bAT[tsl]])
                tsl = ts % 2
                dst = a_hbm[ex, :, :, ts * 128:(ts + 1) * 128].rearrange(
                    "ic p t -> p ic t")
                dep.dma("q", "aw%d" % tsl, lambda e, d=dst, tsl=tsl: e.dma_start(
                    out=d, in_=aT_st[:, tsl, :, :]),
                    reads=[bAT[tsl]], writes=[HB["a"]])
        dep.barrier()

        dep.phase("PD2")
        acc = Ba[:, :].rearrange("p (ts d) -> p ts d", ts=16)
        wdn_v = A[:, :].rearrange("p (sl ic d) -> p sl ic d", sl=2, ic=8)
        bWD = [buf("wdn0"), buf("wdn1")]
        aT_ring = Dn[:, 0:4096].rearrange("p (r ic t) -> p r ic t", r=4, ic=8)
        bATr = [buf("atr%d" % r) for r in range(4)]
        dsem("wdn0"); dsem("wdn1")
        dsem("atr0"); dsem("atr1"); dsem("atr2"); dsem("atr3")

        def load_wdn(e_, sl):
            src_ = wdn_in[e_].rearrange("(ic p) d -> p ic d", p=128)
            dep.dma("g", "wdn%d" % sl, lambda e, s=src_, sl=sl:
                    e.dma_start(out=wdn_v[:, sl], in_=s),
                    writes=[bWD[sl]])

        def load_aT(e_, ts):
            r = ts % 4
            src = a_hbm[e_, :, :, ts * 128:(ts + 1) * 128].rearrange(
                "ic p t -> p ic t")
            dep.dma("q", "atr%d" % r, lambda e, s=src, d=aT_ring[:, r]:
                    e.dma_start(out=d, in_=s),
                    reads=[HB["a"]], writes=[bATr[r]])

        load_wdn(0, 0)
        load_aT(0, 0)
        load_aT(0, 1)
        for ex in range(NE):
            sl = ex % 2
            if ex + 1 < NE:
                load_wdn(ex + 1, 1 - sl)
            for ts in range(16):
                r = ts % 4
                if ts + 2 < 16:
                    load_aT(ex, ts + 2)
                elif ex + 1 < NE:
                    load_aT(ex + 1, ts + 2 - 16)
                pg = (ts % 2) * 4
                for dq in range(4):
                    for ic in range(8):
                        dep.t(lambda e, ic=ic, dq=dq, pg=pg, r=r, sl=sl:
                              e.matmul(
                                  bank(pg + dq), aT_ring[:, r, ic, :],
                                  wdn_v[:, sl, ic, dq * 512:(dq + 1) * 512],
                                  start=(ic == 0), stop=(ic == 7)),
                              reads=[bATr[r], bWD[sl]], writes=[PSB[pg + dq]])
                ps4 = PS[:, pg * 512:(pg + 4) * 512]
                psb4 = [PSB[pg + i] for i in range(4)]
                if ex == 0:
                    dep.v(lambda e, ps4=ps4, ts=ts: e.tensor_copy(
                        acc[:, ts, :], ps4),
                        reads=psb4, writes=[bB])
                else:
                    dep.v(lambda e, ps4=ps4, ts=ts: e.tensor_tensor(
                        acc[:, ts, :], ps4, acc[:, ts, :], ALU.add),
                        reads=psb4 + [bB], writes=[bB])
        dep.barrier()

        dep.phase("PE")
        wo_dq = A[:, 0:8192].rearrange("p (fc x) -> p fc x", fc=16)
        shd_dq = A[:, 8192:24576].rearrange("p (ic x) -> p ic x", ic=32)
        yst = A[:, 24576:26624].bitcast(F32).rearrange("p (sl x) -> p sl x",
                                                       sl=2)
        bWO, bSHD = buf("wo"), buf("shd")
        bYS = [buf("ys0"), buf("ys1")]
        ctx_ring = Dn[:, 0:8192].rearrange("p (r fc t) -> p r fc t", r=4, fc=16)
        asT_ring = Dn[:, 8192:16384].rearrange("p (r ic t) -> p r ic t",
                                               r=2, ic=32)
        bCXr = [buf("cxr%d" % r) for r in range(4)]
        bASr = [buf("asr%d" % r) for r in range(2)]
        bXr = [buf("xr0"), buf("xr1")]
        dep.dma("q", "bo", lambda e: e.dma_start(
            out=E[:, :], in_=bo_in[None, :].broadcast_to([128, D])),
            reads=[bE], writes=[bE])
        dsem("wo"); dsem("shd"); dsem("bo"); dsem("yw0"); dsem("yw1")
        for _r in range(4): dsem("cxr%d" % _r)
        for _r in range(2): dsem("asr%d" % _r)
        for _r in range(2): dsem("xr%d" % _r)
        dsem("yw")

        woT_r = woT.rearrange("(fc p) d -> p fc d", p=128)
        shdn_r = shdn_in.rearrange("(ic p) d -> p ic d", p=128)

        def load_pe_weights(dq):
            dep.dma("g", "wo", lambda e, dq=dq: e.dma_start(
                out=wo_dq[:, :, :],
                in_=woT_r[:, :, dq * 512:(dq + 1) * 512]), writes=[bWO])
            dep.dma("g", "shd", lambda e, dq=dq: e.dma_start(
                out=shd_dq[:, :, :],
                in_=shdn_r[:, :, dq * 512:(dq + 1) * 512]), writes=[bSHD])

        def load_pe_ts(ts):
            r, r2 = ts % 4, ts % 2
            src = ctx_hbm[:, :, ts * 128:(ts + 1) * 128].rearrange(
                "h p t -> p h t")
            dep.dma("q", "cxr%d" % r, lambda e, s=src, d=ctx_ring[:, r]:
                    e.dma_start(out=d, in_=s),
                    reads=[HB["ctx"]], writes=[bCXr[r]])
            src2 = as_hbm[:, :, ts * 128:(ts + 1) * 128].rearrange(
                "ic p t -> p ic t")
            dep.dma("q", "asr%d" % r2, lambda e, s=src2, d=asT_ring[:, r2]:
                    e.dma_start(out=d, in_=s),
                    reads=[HB["as"]], writes=[bASr[r2]])

        def load_pe_x(ts, dq):
            cs = ts % 2
            src = x_own[ts * 128:(ts + 1) * 128, dq * 512:(dq + 1) * 512]
            dep.dma("q", "xr%d" % cs, lambda e, s=src,
                    d=C[:, cs * 1024:cs * 1024 + 512]:
                    e.dma_start(out=d, in_=s), writes=[bXr[cs]])

        for dq in range(4):
            load_pe_weights(dq)
            load_pe_ts(0)
            load_pe_x(0, dq)
            for ts in range(16):
                r, r2, cs = ts % 4, ts % 2, ts % 2
                if ts + 1 < 16:
                    load_pe_ts(ts + 1)
                    load_pe_x(ts + 1, dq)
                pb = ts % 2
                for fc in range(16):
                    dep.t(lambda e, fc=fc, pb=pb, r=r: e.matmul(
                        bank(pb), ctx_ring[:, r, fc, :], wo_dq[:, fc, :],
                        start=(fc == 0), stop=False),
                        reads=[bCXr[r], bWO], writes=[PSB[pb]])
                for ic in range(32):
                    dep.t(lambda e, ic=ic, pb=pb, r2=r2: e.matmul(
                        bank(pb), asT_ring[:, r2, ic, :], shd_dq[:, ic, :],
                        start=False, stop=(ic == 31)),
                        reads=[bASr[r2], bSHD], writes=[PSB[pb]])
                ys = yst[:, cs, :]
                dep.v(lambda e, pb=pb, ys=ys, cs=cs: e.tensor_tensor(
                    ys, bank(pb), C[:, cs * 1024:cs * 1024 + 512], ALU.add),
                    reads=[PSB[pb], bXr[cs]], writes=[bYS[cs]])
                dep.v(lambda e, ys=ys, ts=ts, dq=dq: e.tensor_tensor(
                    ys, ys, acc[:, ts, dq * 512:(dq + 1) * 512], ALU.add),
                    reads=[bB, bYS[cs]], writes=[bYS[cs]])
                dep.v(lambda e, ys=ys, dq=dq: e.tensor_tensor(
                    ys, ys, E[:, dq * 512:(dq + 1) * 512], ALU.add),
                    reads=[bE, bYS[cs]], writes=[bYS[cs]])
                dst = y_out[ts * 128:(ts + 1) * 128, dq * 512:(dq + 1) * 512]
                dep.dma("q", "yw%d" % cs, lambda e, d=dst, s=ys:
                        e.dma_start(out=d, in_=s),
                        reads=[bYS[cs]], writes=[buf("yout")])

        for name, dst in tap_outs.items():
            src = taps[name]
            n0 = src.tensor.shape[0]
            for i in range(n0):
                sflat = src[i].rearrange(
                    " ".join("abc"[:len(src.tensor.shape) - 1]) + " -> ("
                    + " ".join("abc"[:len(src.tensor.shape) - 1]) + ")")
                dflat = dst[i].rearrange(
                    " ".join("abc"[:len(dst.tensor.shape) - 1]) + " -> ("
                    + " ".join("abc"[:len(dst.tensor.shape) - 1]) + ")")
                dep.dma("q", "yw", lambda e, s=sflat, d=dflat: e.dma_start(
                    out=d[None, :], in_=s[None, :]),
                    reads=[HB[tapbuf[name]]], writes=[buf("tap")])

        dep.barrier()

        with nc.Block() as block:
            allsems = [bar_sem, boot_sem] + list(sems.values()) \
                + list(dma_sems.values())
            nums = sorted(s.num for s in allsems)
            assert nums[0] == bar_sem.num
            assert nums[-1] - nums[0] + 1 == len(nums), "sems not contiguous"
            pad_aps = {"v": (ln1[:, 14:15], ln1[:, 15:16]),
                       "s": (ln1[:, 12:13], ln1[:, 13:14])}
            if os.environ.get("KDBG_NOPAD"):
                pad_aps = None
            dep.emit(block, sems, dma_sems, bar_sem, boot_sem,
                     range(nums[0], nums[-1] + 1), pad_aps, nc)

    return nc


_BUILT = {}


def _get_program(debug_taps=()):
    key = tuple(sorted(debug_taps))
    if key not in _BUILT:
        _BUILT[key] = build_program(key)
    return _BUILT[key]


def _own_rows(p):
    idx = np.arange(OWN)
    i, r = idx // 128, idx % 128
    return (2 * i + p) * 128 + r


def _cmask_np(p):
    w = np.arange(1024)[None, :]
    r = np.arange(128)[:, None]
    out = np.stack([np.where(w <= 256 * qt + 128 * p + r, 0.0, NEG)
                    for qt in range(4)])
    return out.astype(ml_dtypes.bfloat16)


def kernel(L0_attn_x, L0_attn_Wq, L0_attn_bq, L0_attn_Wk, L0_attn_bk,
           L0_attn_Wv, L0_attn_bv, L0_attn_Wo, L0_attn_bo,
           L0_ffn_router_weight, L0_ffn_e_bias, L0_ffn_w_up, L0_ffn_w_down,
           L0_ffn_shared_up, L0_ffn_shared_down, _debug_taps=()):
    f32 = np.float32
    x = np.ascontiguousarray(np.asarray(L0_attn_x, dtype=f32))
    shared = {
        "wqT": np.ascontiguousarray(np.asarray(L0_attn_Wq, dtype=f32).T),
        "wkT": np.ascontiguousarray(np.asarray(L0_attn_Wk, dtype=f32).T),
        "wvT": np.ascontiguousarray(np.asarray(L0_attn_Wv, dtype=f32).T),
        "woT": np.ascontiguousarray(np.asarray(L0_attn_Wo, dtype=f32).T),
        "wrT": np.ascontiguousarray(
            np.asarray(L0_ffn_router_weight, dtype=f32).T),
        "bq": np.asarray(L0_attn_bq, dtype=f32),
        "bk": np.asarray(L0_attn_bk, dtype=f32),
        "bv": np.asarray(L0_attn_bv, dtype=f32),
        "bo": np.asarray(L0_attn_bo, dtype=f32),
        "ebias": np.asarray(L0_ffn_e_bias, dtype=f32),
        "wup": np.ascontiguousarray(np.asarray(L0_ffn_w_up, dtype=f32)),
        "wdn": np.ascontiguousarray(np.asarray(L0_ffn_w_down, dtype=f32)),
        "shup": np.ascontiguousarray(np.asarray(L0_ffn_shared_up, dtype=f32)),
        "shdn": np.ascontiguousarray(np.asarray(L0_ffn_shared_down, dtype=f32)),
    }
    cmasks = [_cmask_np(0), _cmask_np(1)]
    rows = [_own_rows(0), _own_rows(1)]
    nc = _get_program(_debug_taps)
    in_maps = []
    for c in range(8):
        b, p = c // 2, c % 2
        m = dict(shared)
        m["x_full"] = x[b]
        m["x_own"] = np.ascontiguousarray(x[b][rows[p]])
        m["cmask"] = cmasks[p]
        in_maps.append(m)
    ncore = int(os.environ.get("KDBG_CORES", "8"))
    res = bass_utils.run_bass_kernel_spmd(nc, in_maps[:ncore],
                                          core_ids=list(range(ncore)))
    out = np.zeros((B, S, D), dtype=f32)
    for c in range(ncore):
        b, p = c // 2, c % 2
        out[b][rows[p]] = res.results[c]["y"]
    if _debug_taps:
        return out, res
    return out


# version 16 (final)
# speedup vs baseline: 1.0325x; 1.0325x over previous
import os
import numpy as np
import ml_dtypes
import concourse.bass as bass
import concourse.mybir as mybir
from concourse import bass_utils

F32 = mybir.dt.float32
BF16 = mybir.dt.bfloat16
AF = mybir.ActivationFunctionType
ALU = mybir.AluOpType
AX = mybir.AxisListType

B, S, D, H, DK = 4, 4096, 2048, 16, 128
NE, MI, SI = 16, 1024, 4096
OWN = 2048
NDC = D // 128
INV_SQRT_DK = 1.0 / float(np.sqrt(DK))
NEG = -1e30
LN_EPS = 1e-5


class Buf:
    __slots__ = ("name", "writers", "readers")

    def __init__(self, name=""):
        self.name = name
        self.writers = {}
        self.readers = {}


class Dep:
    ENGS = ("t", "v", "s", "q", "g")

    def __init__(self):
        self.ops = {e: [] for e in self.ENGS}
        self.dma_count = {}
        self.nbar = 0
        self.cur_phase = "init"

    def phase(self, name):
        self.cur_phase = name

    def _record(self, eng, fn, reads, writes, dma_sem=None):
        idx = len(self.ops[eng])
        waits = {}
        pad = False

        def need(key, val):
            if key[0] == "E" and key[1] == eng:
                return
            if waits.get(key, -1) < val:
                waits[key] = val

        for b in reads:
            for k, v in b.writers.items():
                need(k, v)
                if k == ("E", eng) and v == idx - 1 and eng in ("v", "s"):
                    pad = True
        for b in writes:
            for k, v in b.writers.items():
                need(k, v)
            for k, v in b.readers.items():
                need(k, v)
        for (kind, e2), v in list(waits.items()):
            if kind == "E":
                self.ops[e2][v]["inc"] = True
        self.ops[eng].append({"fn": fn, "waits": waits, "inc": False,
                              "dma_sem": dma_sem, "pad": pad,
                              "ph": self.cur_phase})
        if dma_sem is not None:
            self.dma_count[dma_sem] = self.dma_count.get(dma_sem, 0) + 1
            val = 16 * self.dma_count[dma_sem]
            for b in writes:
                b.writers[("D", dma_sem)] = val
            for b in reads:
                b.readers[("D", dma_sem)] = val
        else:
            for b in writes:
                b.writers[("E", eng)] = idx
            for b in reads:
                b.readers[("E", eng)] = idx
        return idx

    def t(self, fn, reads=(), writes=()):
        return self._record("t", fn, reads, writes)

    def v(self, fn, reads=(), writes=()):
        return self._record("v", fn, reads, writes)

    def s(self, fn, reads=(), writes=()):
        return self._record("s", fn, reads, writes)

    def dma(self, queue, sem, fn, reads=(), writes=()):
        return self._record(queue, fn, reads, writes, dma_sem=sem)

    def seal(self, sem, bufs):
        v = 16 * self.dma_count.get(sem, 0)
        for b in bufs:
            if ("D", sem) in b.writers:
                b.writers[("D", sem)] = v

    def barrier(self):
        self.nbar += 1
        for e in self.ENGS:
            self.ops[e].append({"bar": True})

    BOOT = 1 << 20

    def emit(self, block, sems, dma_sems, bar_sem, boot_sem, clear_range,
             pad_aps=None, nc=None):
        val_of = {}
        for e in self.ENGS:
            acc, m = 0, {}
            for i, op in enumerate(self.ops[e]):
                if not op.get("bar") and op.get("inc"):
                    acc += 1
                m[i] = acc
            val_of[e] = m
        flush = {}
        for e in ("q", "g"):
            c, totals = {}, []
            for op in self.ops[e]:
                if op.get("bar"):
                    totals.append(dict(c))
                elif op["dma_sem"] is not None:
                    c[op["dma_sem"]] = c.get(op["dma_sem"], 0) + 1
            totals.append(dict(c))
            flush[e] = totals

        def emit_stream(ek, ebass):
            waited, bar_i = {}, 0
            is_q = ek in ("q", "g")
            cur_ph = [None]
            scope = [None]

            def set_ph(p):
                if p == cur_ph[0] or nc is None:
                    return
                if scope[0] is not None:
                    scope[0].__exit__(None, None, None)
                cur_ph[0] = p
                scope[0] = nc.named_scope(p) if p else None
                if scope[0] is not None:
                    scope[0].__enter__()
            import os as _os
            if _os.environ.get("KDBG_NOBOOT") or _os.environ.get("KDBG_NOSTART"):
                pass
            elif ek == "g":
                if not _os.environ.get("KDBG_NODMARESET"):
                    ebass.dma_reset(clear_range)
                ebass.sem_clear(clear_range)
                ebass.sem_inc(boot_sem, self.BOOT)
            else:
                ebass.wait_ge(boot_sem, self.BOOT)
            for op in self.ops[ek]:
                if op.get("bar"):
                    set_ph(None)
                    if is_q:
                        for sm, c in flush[ek][bar_i].items():
                            h = dma_sems[sm]
                            if waited.get(h, -1) < 16 * c:
                                ebass.wait_ge(h, 16 * c)
                                waited[h] = 16 * c
                    bar_i += 1
                    ebass.nop().then_inc(bar_sem, 1)
                    ebass.wait_ge(bar_sem, 5 * bar_i)
                    continue
                for key, v in sorted(op["waits"].items(), key=str):
                    if key[0] == "E":
                        h, val = sems[key[1]], val_of[key[1]][v]
                        assert val >= 1
                    else:
                        h, val = dma_sems[key[1]], v
                    if waited.get(h, -1) < val:
                        ebass.wait_ge(h, val)
                        waited[h] = val
                set_ph(op.get("ph"))
                if op.get("pad") and pad_aps and ek in pad_aps:
                    a, b = pad_aps[ek]
                    if ek == "v":
                        ebass.tensor_copy(a, b)
                    else:
                        ebass.copy(a, b)
                ins = op["fn"](ebass)
                if op["dma_sem"] is not None:
                    ins.then_inc(dma_sems[op["dma_sem"]], 16)
                elif op.get("inc"):
                    ins.then_inc(sems[ek], 1)
            set_ph(None)
            if ek == "g" and not (_os.environ.get("KDBG_NOBOOT")
                                  or _os.environ.get("KDBG_NOEND")):
                ebass.sem_clear(range(boot_sem.num, boot_sem.num + 1))

        block.tensor(lambda e: emit_stream("t", e))
        block.vector(lambda e: emit_stream("v", e))
        block.scalar(lambda e: emit_stream("s", e))
        block.sync(lambda e: emit_stream("q", e))
        block.gpsimd(lambda e: emit_stream("g", e))


def build_program(debug_taps=()):
    nc = bass.Bass("TRN2", target_bir_lowering=False, debug=False)

    x_full = nc.dram_tensor("x_full", [S, D], F32, kind="ExternalInput").ap()
    x_own = nc.dram_tensor("x_own", [OWN, D], F32, kind="ExternalInput").ap()
    cmask_in = nc.dram_tensor("cmask", [4, 128, 1024], BF16,
                              kind="ExternalInput").ap()
    wqT = nc.dram_tensor("wqT", [D, D], F32, kind="ExternalInput").ap()
    wkT = nc.dram_tensor("wkT", [D, D], F32, kind="ExternalInput").ap()
    wvT = nc.dram_tensor("wvT", [D, D], F32, kind="ExternalInput").ap()
    woT = nc.dram_tensor("woT", [D, D], F32, kind="ExternalInput").ap()
    bq_in = nc.dram_tensor("bq", [D], F32, kind="ExternalInput").ap()
    bk_in = nc.dram_tensor("bk", [D], F32, kind="ExternalInput").ap()
    bv_in = nc.dram_tensor("bv", [D], F32, kind="ExternalInput").ap()
    bo_in = nc.dram_tensor("bo", [D], F32, kind="ExternalInput").ap()
    wrT = nc.dram_tensor("wrT", [D, NE], F32, kind="ExternalInput").ap()
    ebias_in = nc.dram_tensor("ebias", [NE], F32, kind="ExternalInput").ap()
    wup_in = nc.dram_tensor("wup", [NE, D, MI], F32, kind="ExternalInput").ap()
    wdn_in = nc.dram_tensor("wdn", [NE, MI, D], F32, kind="ExternalInput").ap()
    shup_in = nc.dram_tensor("shup", [D, SI], F32, kind="ExternalInput").ap()
    shdn_in = nc.dram_tensor("shdn", [SI, D], F32, kind="ExternalInput").ap()
    y_out = nc.dram_tensor("y", [OWN, D], F32, kind="ExternalOutput").ap()

    k_hbm = nc.dram_tensor("k_hbm", [H, 128, S], BF16).ap()
    v_hbm = nc.dram_tensor("v_hbm", [H, S // 128, 128, 128], BF16).ap()
    xn_own_hbm = nc.dram_tensor("xn_own_hbm", [NDC, 128, OWN], BF16).ap()
    q_hbm = nc.dram_tensor("q_hbm", [H, 128, OWN], BF16).ap()
    ctx_hbm = nc.dram_tensor("ctx_hbm", [H, 128, OWN], BF16).ap()
    a_hbm = nc.dram_tensor("a_hbm", [NE, MI // 128, 128, OWN], BF16).ap()
    as_hbm = nc.dram_tensor("as_hbm", [SI // 128, 128, OWN], BF16).ap()
    xnf_dbg = (nc.dram_tensor("xnf_dbg", [NDC, 128, S], BF16).ap()
               if "xnf_dbg" in debug_taps else None)

    ident = nc.inline_tensor(np.eye(128, dtype=ml_dtypes.bfloat16),
                             name="ident").ap()
    e_ = np.arange(16)
    lt16_np = (e_[None, :] < e_[:, None]).astype(np.float32).reshape(1, 256)
    g_ = np.arange(4)
    lt4_np = (g_[None, :] < g_[:, None]).astype(np.float32).reshape(1, 16)
    lt16_c = nc.inline_tensor(lt16_np, name="lt16").ap()
    lt4_c = nc.inline_tensor(lt4_np, name="lt4").ap()

    dep = Dep()
    taps = {"k_hbm": k_hbm, "v_hbm": v_hbm, "q_hbm": q_hbm,
            "xn_own_hbm": xn_own_hbm, "ctx_hbm": ctx_hbm, "a_hbm": a_hbm,
            "as_hbm": as_hbm, "xnf_dbg": xnf_dbg}
    tapbuf = {"k_hbm": "k", "v_hbm": "v", "q_hbm": "q", "xn_own_hbm": "xn",
              "ctx_hbm": "ctx", "a_hbm": "a", "as_hbm": "as",
              "xnf_dbg": "xnf"}
    tap_outs = {}
    for name in debug_taps:
        src = taps[name]
        tap_outs[name] = nc.dram_tensor(
            "tap_" + name, list(src.tensor.shape), BF16,
            kind="ExternalOutput").ap()

    import contextlib
    with contextlib.ExitStack() as ctx_mgr:
        def alloc(name, shape, dtype):
            return ctx_mgr.enter_context(nc.sbuf_tensor(name, shape, dtype))

        A = alloc("A", [128, 32768], BF16)
        Ba = alloc("Ba", [128, 32768], BF16)
        C = alloc("C", [128, 4096], F32)
        Dn = alloc("Dn", [128, 16384], BF16)
        E = alloc("E", [128, 2048], F32)
        idsb = alloc("idsb", [128, 128], BF16)
        pTs = alloc("pTs", [128, 2, 512], BF16)
        kst = alloc("kst", [128, 2, 512], BF16)
        trst = alloc("trst", [128, 4096], BF16)
        f2 = alloc("f2", [128, 2048], F32)
        ln1 = alloc("ln1", [128, 16], F32)
        bqkb = alloc("bqkb", [128, 2, 16], F32)
        ebias_bc = alloc("ebias_bc", [128, 16], F32)
        comb = alloc("comb", [128, 256], F32)
        rsc = alloc("rsc", [128, 32], F32)
        lt16_sb = alloc("lt16_sb", [128, 256], F32)
        lt4_sb = alloc("lt4_sb", [128, 16], F32)
        PS = ctx_mgr.enter_context(nc.psum_tensor("PS", [128, 4096], F32))
        assert nc.sbuf_bytes_remaining >= 0

        def bank(b, n=512):
            return PS[:, b * 512:b * 512 + n]

        def bank_bf(b):
            return PS[:, b * 512:(b + 1) * 512].bitcast(BF16)

        bar_sem = ctx_mgr.enter_context(nc.semaphore("sem_bar"))
        boot_sem = ctx_mgr.enter_context(nc.semaphore("sem_boot"))
        sems = {e: ctx_mgr.enter_context(nc.semaphore("sem_" + e))
                for e in Dep.ENGS}
        dma_sems = {}

        def dsem(name):
            if name not in dma_sems:
                dma_sems[name] = ctx_mgr.enter_context(nc.semaphore("d_" + name))
            return name

        bufs = {}

        def buf(name):
            if name not in bufs:
                bufs[name] = Buf(name)
            return bufs[name]

        PSB = [buf(f"ps{i}") for i in range(8)]
        HB = {n: buf("hbm_" + n) for n in
              ("k", "v", "q", "xn", "ctx", "a", "as", "xnf")}
        bA, bB, bE = buf("A"), buf("B"), buf("E")

        for nm in ("cst",):
            dsem(nm)
        dep.dma("q", "cst", lambda e: e.dma_start(out=idsb[:, :], in_=ident[:, :]),
                writes=[buf("idsb")])
        bq_r = bq_in.rearrange("(h p) -> p h", p=128)
        bk_r = bk_in.rearrange("(h p) -> p h", p=128)
        dep.dma("q", "cst", lambda e: e.dma_start(out=bqkb[:, 0, :], in_=bq_r,
                                          allow_slow_non_contiguous=True),
                writes=[buf("bqkb")])
        dep.dma("q", "cst", lambda e: e.dma_start(out=bqkb[:, 1, :], in_=bk_r,
                                          allow_slow_non_contiguous=True),
                writes=[buf("bqkb")])
        dep.dma("q", "cst", lambda e: e.dma_start(
            out=ebias_bc[:, :], in_=ebias_in[None, :].broadcast_to([128, NE])),
            writes=[buf("ebias_bc")])
        dep.dma("q", "cst", lambda e: e.dma_start(
            out=E[:, :], in_=bv_in[None, :].broadcast_to([128, D])),
            writes=[bE])
        dep.dma("q", "cst", lambda e: e.dma_start(
            out=lt16_sb[:, :], in_=lt16_c[:, :].broadcast_to([128, 256])),
            writes=[buf("lt16")])
        dep.dma("q", "cst", lambda e: e.dma_start(
            out=lt4_sb[:, :], in_=lt4_c[:, :].broadcast_to([128, 16])),
            writes=[buf("lt4")])
        dep.dma("q", "cst", lambda e: e.dma_start(
            out=ln1[:, 12:16].bitcast(BF16), in_=ident[:, 0:8]),
            writes=[buf("padseed")])
        dep.seal("cst", [buf("idsb"), buf("bqkb"), buf("ebias_bc"), bE,
                         buf("lt16"), buf("lt4"), buf("padseed")])
        dep.barrier()

        xt_v = C[:, :].rearrange("p (sl d) -> p sl d", sl=2)
        bC = [buf("c0"), buf("c1")]
        xtok = trst[:, 0:2048]
        bXT = buf("xtok")
        bLN = buf("ln1")

        def emit_ln(cs):
            xt = xt_v[:, cs]
            dep.v(lambda e, xt=xt: e.reduce_sum(ln1[:, 0:1], xt, axis=AX.X),
                  reads=[bC[cs]], writes=[bLN])
            dep.s(lambda e: e.mul(ln1[:, 1:2], ln1[:, 0:1], -1.0 / D),
                  reads=[bLN], writes=[bLN])
            dep.v(lambda e, xt=xt: e.tensor_mul(
                PS[:, 3072:4096], xt[:, 0:1024], xt[:, 0:1024]),
                reads=[bC[cs]], writes=[PSB[6], PSB[7]])
            dep.v(lambda e: e.reduce_sum(ln1[:, 2:3], PS[:, 3072:4096],
                                         axis=AX.X),
                  reads=[PSB[6], PSB[7]], writes=[bLN])
            dep.v(lambda e, xt=xt: e.tensor_mul(
                PS[:, 3072:4096], xt[:, 1024:2048], xt[:, 1024:2048]),
                reads=[bC[cs]], writes=[PSB[6], PSB[7]])
            dep.v(lambda e: e.reduce_sum(ln1[:, 3:4], PS[:, 3072:4096],
                                         axis=AX.X),
                  reads=[PSB[6], PSB[7]], writes=[bLN])
            dep.v(lambda e: e.tensor_add(ln1[:, 4:5], ln1[:, 2:3],
                                         ln1[:, 3:4]),
                  reads=[bLN], writes=[bLN])
            dep.v(lambda e: e.tensor_mul(ln1[:, 5:6], ln1[:, 1:2], ln1[:, 1:2]),
                  reads=[bLN], writes=[bLN])
            dep.v(lambda e: e.tensor_scalar(ln1[:, 6:7], ln1[:, 4:5], 1.0 / D,
                                            ln1[:, 5:6], ALU.mult, ALU.subtract),
                  reads=[bLN], writes=[bLN])
            dep.v(lambda e: e.tensor_single_scalar(ln1[:, 7:8], ln1[:, 6:7],
                                               LN_EPS, ALU.add),
                  reads=[bLN], writes=[bLN])
            dep.v(lambda e: e.reciprocal(ln1[:, 8:9], ln1[:, 7:8]),
                  reads=[bLN], writes=[bLN])
            dep.s(lambda e: e.activation(ln1[:, 9:10], ln1[:, 8:9], AF.Sqrt),
                  reads=[bLN], writes=[bLN])
            dep.v(lambda e: e.tensor_mul(ln1[:, 10:11], ln1[:, 1:2], ln1[:, 9:10]),
                  reads=[bLN], writes=[bLN])
            dep.s(lambda e, xt=xt: e.activation(
                xtok[:, :], xt, AF.Identity,
                bias=ln1[:, 10:11], scale=ln1[:, 9:10]),
                reads=[bC[cs], bLN], writes=[bXT])

        def emit_transpose16(dst_view, dst_buf):
            for half8 in range(2):
                pb = 4 + half8
                for j in range(8):
                    dc = half8 * 8 + j
                    dep.t(lambda e, dc=dc, pb=pb, j=j: e.transpose(
                        bank_bf(pb)[:, j * 128:(j + 1) * 128],
                        xtok[:, dc * 128:(dc + 1) * 128], idsb[:, :]),
                        reads=[bXT, buf("idsb")], writes=[PSB[pb]])
                dep.v(lambda e, pb=pb, half8=half8: e.tensor_copy(
                    dst_view[:, half8 * 8:(half8 + 1) * 8, :],
                    bank_bf(pb).rearrange("p (a b) -> p a b", a=8)),
                    reads=[PSB[pb]], writes=[dst_buf])

        dep.phase("P0")
        wk_bf = A[:, :].rearrange("p (dc e) -> p dc e", dc=NDC)
        wv_bf = Ba[:, :].rearrange("p (dc e) -> p dc e", dc=NDC)
        dsem("wk"); dsem("wv"); dsem("xld0"); dsem("xld1"); dsem("xfd")
        dsem("kw0"); dsem("kw1"); dsem("vw0"); dsem("vw1")
        wkT_r = wkT.rearrange("(dc p) e -> p dc e", p=128)
        wvT_r = wvT.rearrange("(dc p) e -> p dc e", p=128)
        dep.dma("g", "wk", lambda e: e.dma_start(out=wk_bf[:, :, :], in_=wkT_r),
                writes=[bA])
        dep.dma("g", "wv", lambda e: e.dma_start(out=wv_bf[:, :, :], in_=wvT_r),
                writes=[bB])

        xn_ch = Dn[:, :].rearrange("p (sl dc t) -> p sl dc t", sl=2, dc=NDC)
        bXN = [buf("xn0"), buf("xn1")]
        bKst = [buf("kst0"), buf("kst1")]

        for cc in range(S // 512):
            sl = cc % 2
            ch = xn_ch[:, sl]
            bx = bXN[sl]
            for tl in range(4):
                t0 = cc * 512 + tl * 128
                cs = tl % 2
                dep.dma("q", "xld%d" % cs, lambda e, s=x_full[t0:t0 + 128, :],
                        d=xt_v[:, cs]: e.dma_start(out=d, in_=s),
                        writes=[bC[cs]])
                emit_ln(cs)
                emit_transpose16(ch[:, :, tl * 128:(tl + 1) * 128], bx)
            if xnf_dbg is not None:
                dstd = xnf_dbg[:, :, cc * 512:(cc + 1) * 512].rearrange(
                    "dc p t -> p dc t")
                dep.dma("q", "xfd", lambda e, d=dstd, s=ch:
                        e.dma_start(out=d, in_=s),
                        reads=[bx], writes=[HB["xnf"]])
            for ec in range(H):
                pb = ec % 2
                for dc in range(NDC):
                    dep.t(lambda e, dc=dc, ec=ec, pb=pb, ch=ch: e.matmul(
                        bank(pb), wk_bf[:, dc, ec * 128:(ec + 1) * 128],
                        ch[:, dc, :], start=(dc == 0), stop=(dc == NDC - 1)),
                        reads=[bA, bx], writes=[PSB[pb]])
                ks = kst[:, pb, :]
                dep.s(lambda e, pb=pb, ks=ks, ec=ec: e.activation(
                    ks, bank(pb), AF.Identity, bias=bqkb[:, 1, ec:ec + 1]),
                    reads=[PSB[pb], buf("bqkb")], writes=[bKst[pb]])
                dep.dma("q", "kw%d" % pb, lambda e,
                        d=k_hbm[ec, :, cc * 512:(cc + 1) * 512], s=ks:
                        e.dma_start(out=d, in_=s),
                        reads=[bKst[pb]], writes=[HB["k"]])
            for tl in range(4):
                for eq in range(4):
                    pb = 2 + (eq % 2)
                    for dc in range(NDC):
                        dep.t(lambda e, dc=dc, tl=tl, eq=eq, pb=pb, ch=ch:
                              e.matmul(
                                  bank(pb), ch[:, dc, tl * 128:(tl + 1) * 128],
                                  wv_bf[:, dc, eq * 512:(eq + 1) * 512],
                                  start=(dc == 0), stop=(dc == NDC - 1)),
                              reads=[bB, bx], writes=[PSB[pb]])
                    vs = kst[:, pb - 2, :]
                    dep.v(lambda e, pb=pb, vs=vs, eq=eq: e.tensor_tensor(
                        vs, bank(pb), E[:, eq * 512:(eq + 1) * 512], ALU.add),
                        reads=[PSB[pb], bE], writes=[bKst[pb - 2]])
                    dst = v_hbm[eq * 4:(eq + 1) * 4, cc * 4 + tl, :, :].rearrange(
                        "h tp dk -> tp h dk")
                    dep.dma("q", "vw%d" % (pb - 2), lambda e, d=dst, s=vs:
                            e.dma_start(out=d, in_=s),
                            reads=[bKst[pb - 2]], writes=[HB["v"]])
        dep.barrier()

        dep.phase("PQ")
        wq_bf = A[:, :].rearrange("p (dc e) -> p dc e", dc=NDC)
        xn_ownB = Ba[:, :].rearrange("p (dc t) -> p dc t", dc=NDC)
        dsem("wq"); dsem("xnw"); dsem("qw0"); dsem("qw1")
        wqT_r = wqT.rearrange("(dc p) e -> p dc e", p=128)
        dep.dma("g", "wq", lambda e: e.dma_start(out=wq_bf[:, :, :], in_=wqT_r),
                writes=[bA])
        for ot in range(16):
            cs = ot % 2
            dep.dma("q", "xld%d" % cs, lambda e,
                    s=x_own[ot * 128:(ot + 1) * 128, :],
                    d=xt_v[:, cs]: e.dma_start(out=d, in_=s), writes=[bC[cs]])
            emit_ln(cs)
            emit_transpose16(xn_ownB[:, :, ot * 128:(ot + 1) * 128], bB)
            if ot % 4 == 3:
                oc = ot // 4
                dst = xn_own_hbm[:, :, oc * 512:(oc + 1) * 512].rearrange(
                    "dc p t -> p dc t")
                dep.dma("q", "xnw", lambda e, d=dst,
                        s=xn_ownB[:, :, oc * 512:(oc + 1) * 512]:
                        e.dma_start(out=d, in_=s),
                        reads=[bB], writes=[HB["xn"]])
        for ec in range(H):
            for oc in range(4):
                pb = oc % 2
                for dc in range(NDC):
                    dep.t(lambda e, dc=dc, ec=ec, oc=oc, pb=pb: e.matmul(
                        bank(pb), wq_bf[:, dc, ec * 128:(ec + 1) * 128],
                        xn_ownB[:, dc, oc * 512:(oc + 1) * 512],
                        start=(dc == 0), stop=(dc == NDC - 1)),
                        reads=[bA, bB], writes=[PSB[pb]])
                qs = kst[:, pb, :]
                dep.s(lambda e, pb=pb, qs=qs, ec=ec: e.activation(
                    qs, bank(pb), AF.Identity, bias=bqkb[:, 0, ec:ec + 1]),
                    reads=[PSB[pb], buf("bqkb")], writes=[bKst[pb]])
                dep.dma("q", "qw%d" % pb, lambda e,
                        d=q_hbm[ec, :, oc * 512:(oc + 1) * 512], s=qs:
                        e.dma_start(out=d, in_=s),
                        reads=[bKst[pb]], writes=[HB["q"]])
        dep.barrier()

        dep.phase("PA")
        Kv = A[:, :].rearrange("p (sl x) -> p sl x", sl=4)
        ctxst = f2[:, 0:512].bitcast(BF16).rearrange("p (sl x) -> p sl x", sl=2)
        Ksl = [Kv[:, 0, 0:4096], Kv[:, 0, 4096:8192]]
        Vsl = [Kv[:, 1, 0:4096].rearrange("p (kc dk) -> p kc dk", kc=32),
               Kv[:, 1, 4096:8192].rearrange("p (kc dk) -> p kc dk", kc=32)]
        Qsl = [Kv[:, 2, 0:2048], Kv[:, 2, 2048:4096]]
        probs = [Ba[:, i * 4096:(i + 1) * 4096] for i in range(4)]
        scors = [[Dn[:, i * 4096:(i + 1) * 4096] for i in range(4)],
                 [A[:, 20480:24576], A[:, 24576:28672], A[:, 28672:32768],
                  Ba[:, 20480:24576]]]
        maskb = Ba[:, 24576:28672].rearrange("p (qt w) -> p qt w", qt=4)
        bb = {n: buf("pa_" + n) for n in
              ("k0", "k1", "v0", "v1", "q0", "q1", "p0", "p1", "p2", "p3",
               "s00", "s10", "s20", "s30", "s01", "s11", "s21", "s31",
               "mask", "rsc0", "rsc1", "pT0", "pT1", "cx0", "cx1")}
        dsem("kr0"); dsem("kr1"); dsem("vr0"); dsem("vr1")
        dsem("qr0"); dsem("qr1"); dsem("cxw0"); dsem("cxw1"); dsem("msk")
        dsem("pT0"); dsem("pT1")
        dep.dma("q", "msk", lambda e: e.dma_start(
            out=maskb[:, :, :], in_=cmask_in.rearrange("qt p w -> p qt w")),
            writes=[bb["mask"]])

        def load_head(h):
            sl = h % 2
            dep.dma("q", "kr%d" % sl, lambda e, s=k_hbm[h], d=Ksl[sl]:
                    e.dma_start(out=d, in_=s),
                    reads=[HB["k"]], writes=[bb["k%d" % sl]])
            src = v_hbm[h].rearrange("kc tp dk -> tp kc dk")
            dep.dma("q", "vr%d" % sl, lambda e, s=src, d=Vsl[sl]:
                    e.dma_start(out=d, in_=s),
                    reads=[HB["v"]], writes=[bb["v%d" % sl]])
            dep.dma("q", "qr%d" % sl, lambda e, s=q_hbm[h], d=Qsl[sl]:
                    e.dma_start(out=d, in_=s),
                    reads=[HB["q"]], writes=[bb["q%d" % sl]])

        def pa_scores(h, g):
            sl = h % 2
            par = g % 2
            bk_, bq_ = bb["k%d" % sl], bb["q%d" % sl]
            K_, Q_ = Ksl[sl], Qsl[sl]
            kg = 1024 * (g + 1)
            nkc = kg // 512
            sb_banks = (0, 1, 5, 6)
            for qt in range(4):
                own0 = g * 512 + qt * 128
                for kc in range(nkc):
                    pb = sb_banks[(qt * nkc + kc) % 4]
                    dep.t(lambda e, kc=kc, pb=pb, own0=own0, Q_=Q_, K_=K_:
                          e.matmul(
                              bank(pb), Q_[:, own0:own0 + 128],
                              K_[:, kc * 512:(kc + 1) * 512],
                              start=True, stop=True),
                          reads=[bq_, bk_], writes=[PSB[pb]])
                    if kc >= nkc - 2:
                        w0 = (kc - (nkc - 2)) * 512
                        dep.v(lambda e, kc=kc, pb=pb, qt=qt, w0=w0, par=par:
                              e.tensor_tensor(
                                  scors[par][qt][:, kc * 512:(kc + 1) * 512],
                                  bank(pb), maskb[:, qt, w0:w0 + 512],
                                  ALU.add),
                              reads=[PSB[pb], bb["mask"]],
                              writes=[bb["s%d%d" % (qt, par)]])
                    else:
                        dep.v(lambda e, kc=kc, pb=pb, qt=qt, par=par:
                              e.tensor_copy(
                                  scors[par][qt][:, kc * 512:(kc + 1) * 512],
                                  bank(pb)),
                              reads=[PSB[pb]], writes=[bb["s%d%d" % (qt, par)]])

        def pa_exps(h, g):
            kg = 1024 * (g + 1)
            par = g % 2
            for qt in range(4):
                dep.s(lambda e, kg=kg, qt=qt, par=par: e.activation(
                    scors[par][qt][:, 0:kg], scors[par][qt][:, 0:kg], AF.Exp,
                    bias=0.0, scale=INV_SQRT_DK),
                    reads=[bb["s%d%d" % (qt, par)]],
                    writes=[bb["s%d%d" % (qt, par)]])

        def pa_sums(h, g):
            kg = 1024 * (g + 1)
            par = g % 2

            def rsum(qt):
                dep.v(lambda e, kg=kg, qt=qt, par=par: e.reduce_sum(
                    rsc[:, 16*par+8+qt:16*par+9+qt],
                    scors[par][qt][:, 0:kg], axis=AX.X),
                    reads=[bb["s%d%d" % (qt, par)]],
                    writes=[bb["rsc%d" % par]])

            def recip(qt):
                dep.v(lambda e, qt=qt, par=par: e.reciprocal(
                    rsc[:, 16*par+12+qt:16*par+13+qt],
                    rsc[:, 16*par+8+qt:16*par+9+qt]),
                    reads=[bb["rsc%d" % par]], writes=[bb["rsc%d" % par]])

            def div(qt):
                dep.v(lambda e, kg=kg, qt=qt, par=par: e.tensor_scalar(
                    scors[par][qt][:, 0:kg], scors[par][qt][:, 0:kg],
                    rsc[:, 16*par+12+qt:16*par+13+qt], None, ALU.mult),
                    reads=[bb["rsc%d" % par], bb["s%d%d" % (qt, par)]],
                    writes=[bb["s%d%d" % (qt, par)]])

            rsum(0); rsum(1); recip(0); rsum(2); recip(1); div(0)
            rsum(3); recip(2); div(1); recip(3); div(2); div(3)

        def pa_trav(h, g):
            sl = h % 2
            par = g % 2
            bv_ = bb["v%d" % sl]
            V_ = Vsl[sl]
            kg = 1024 * (g + 1)
            nk = kg // 128

            def emit_trdma(j):
                psl = j % 2
                pb = 2 + psl
                for qt in range(4):
                    dep.t(lambda e, j=j, qt=qt, pb=pb, par=par: e.transpose(
                        bank_bf(pb)[:, qt * 128:(qt + 1) * 128],
                        scors[par][qt][:, j * 128:(j + 1) * 128], idsb[:, :]),
                        reads=[bb["s%d%d" % (qt, par)], buf("idsb")],
                        writes=[PSB[pb]])
                dep.s(lambda e, pb=pb, psl=psl: e.copy(
                    pTs[:, psl, :], bank_bf(pb)[:, 0:512]),
                    reads=[PSB[pb]], writes=[bb["pT%d" % psl]])

            def emit_av(j):
                dep.t(lambda e, j=j, nk=nk, V_=V_: e.matmul(
                    bank(4), V_[:, j, :], pTs[:, j % 2, :],
                    start=(j == 0), stop=(j == nk - 1)),
                    reads=[bv_, bb["pT%d" % (j % 2)]], writes=[PSB[4]])

            emit_trdma(0)
            for j in range(1, nk):
                emit_trdma(j)
                emit_av(j - 1)
            emit_av(nk - 1)

        def pa_ctx(h, g):
            csl = g % 2
            cst_ = ctxst[:, csl, :]
            dep.v(lambda e, cst_=cst_: e.tensor_copy(cst_, bank(4)),
                  reads=[PSB[4]], writes=[bb["cx%d" % csl]])
            dep.dma("q", "cxw%d" % csl, lambda e, h=h, g=g, cst_=cst_:
                    e.dma_start(
                        out=ctx_hbm[h, :, g * 512:(g + 1) * 512], in_=cst_),
                    reads=[bb["cx%d" % csl]], writes=[HB["ctx"]])

        load_head(0)
        pa_scores(0, 0)
        seq = [(h, g) for h in range(H) for g in range(4)]
        for i, (h, g) in enumerate(seq):
            pa_exps(h, g)
            pa_sums(h, g)
            if i > 0:
                pa_trav(*seq[i - 1])
            if g == 0 and h + 1 < H:
                load_head(h + 1)
            if i + 1 < len(seq):
                pa_scores(*seq[i + 1])
            if i > 0:
                pa_ctx(*seq[i - 1])
        pa_trav(*seq[-1])
        pa_ctx(*seq[-1])
        dep.barrier()

        dep.phase("PC1")
        shup_lo = A[:, :].rearrange("p (dc i) -> p dc i", dc=8)
        shup_hi = Ba[:, :].rearrange("p (dc i) -> p dc i", dc=8)

        def shup_v(dc):
            return shup_lo[:, dc] if dc < 8 else shup_hi[:, dc - 8]

        dsem("shu"); dsem("asw"); dsem("xnr0"); dsem("xnr1")
        shup_r = shup_in.rearrange("(dc p) i -> p dc i", p=128)
        dep.dma("g", "shu", lambda e: e.dma_start(
            out=shup_lo[:, :, :], in_=shup_r[:, 0:8]), writes=[bA])
        dep.dma("g", "shu", lambda e: e.dma_start(
            out=shup_hi[:, :, :], in_=shup_r[:, 8:16]), writes=[bB])
        xn_ch2 = Dn[:, :].rearrange("p (sl dc t) -> p sl dc t", sl=2, dc=NDC)
        asT = trst[:, :].rearrange("p (ic t) -> p ic t", ic=32)
        bAS = buf("asT")
        a_q = f2[:, 0:512].bitcast(BF16).rearrange("p (sl x) -> p sl x", sl=2)
        rtmp = f2[:, 1024:2048].rearrange("p (sl x) -> p sl x", sl=2)
        bAQ = [buf("aq0"), buf("aq1")]
        bRT = [buf("rt0"), buf("rt1")]

        for oc in range(4):
            sl = oc % 2
            ch = xn_ch2[:, sl]
            bx = bXN[sl]
            src = xn_own_hbm[:, :, oc * 512:(oc + 1) * 512].rearrange(
                "dc p t -> p dc t")
            dep.dma("q", "xnr%d" % sl, lambda e, s=src, d=ch:
                    e.dma_start(out=d, in_=s),
                    reads=[HB["xn"]], writes=[bx])
            for k in range(4):
                ts = oc * 4 + k
                for ih in range(8):
                    pb = ih % 2
                    for dc in range(NDC):
                        dep.t(lambda e, dc=dc, k=k, ih=ih, pb=pb, ch=ch:
                              e.matmul(
                                  bank(pb), ch[:, dc, k * 128:(k + 1) * 128],
                                  shup_v(dc)[:, ih * 512:(ih + 1) * 512],
                                  start=(dc == 0), stop=(dc == NDC - 1)),
                              reads=[bA, bB, bx], writes=[PSB[pb]])
                    rt = rtmp[:, pb, :]
                    dep.s(lambda e, pb=pb, rt=rt: e.activation(
                        rt, bank(pb), AF.Relu),
                        reads=[PSB[pb]], writes=[bRT[pb]])
                    aq = a_q[:, pb, :]
                    dep.v(lambda e, pb=pb, rt=rt, aq=aq: e.tensor_tensor(
                        aq, bank(pb), rt, ALU.mult),
                        reads=[PSB[pb], bRT[pb]], writes=[bAQ[pb]])
                    tb = 2 + pb
                    for jj in range(4):
                        dep.t(lambda e, jj=jj, tb=tb, aq=aq: e.transpose(
                            bank_bf(tb)[:, jj * 128:(jj + 1) * 128],
                            aq[:, jj * 128:(jj + 1) * 128], idsb[:, :]),
                            reads=[bAQ[tb - 2], buf("idsb")], writes=[PSB[tb]])
                    dep.v(lambda e, tb=tb, ih=ih: e.tensor_copy(
                        asT[:, ih * 4:(ih + 1) * 4, :].rearrange(
                            "p a b -> p (a b)"),
                        bank_bf(tb)[:, 0:512]),
                        reads=[PSB[tb]], writes=[bAS])
                dst = as_hbm[:, :, ts * 128:(ts + 1) * 128].rearrange(
                    "ic p t -> p ic t")
                dep.dma("q", "asw", lambda e, d=dst: e.dma_start(
                    out=d, in_=asT[:, :, :]),
                    reads=[bAS], writes=[HB["as"]])
        dep.barrier()

        dep.phase("PR")
        xn_own = A[:, :].rearrange("p (dc t) -> p dc t", dc=NDC)
        for oc in range(4):
            src = xn_own_hbm[:, :, oc * 512:(oc + 1) * 512].rearrange(
                "dc p t -> p dc t")
            dst = xn_own[:, :, oc * 512:(oc + 1) * 512]
            dep.dma("q", "xnrA", lambda e, s=src, d=dst: e.dma_start(out=d, in_=s),
                    reads=[HB["xn"]], writes=[bA])
        wr_bf = Ba[:, 0:256].rearrange("p (dc e) -> p dc e", dc=NDC)
        wr_src = wrT.rearrange("(dc p) e -> p dc e", p=128)
        dsem("wr"); dsem("xnrA")
        dep.dma("g", "wr", lambda e: e.dma_start(out=wr_bf[:, :, :], in_=wr_src),
                writes=[bB])
        fR = f2
        bF2 = buf("f2")
        bCMB = buf("comb")
        for ts in range(16):
            pb = ts % 2
            for dc in range(NDC):
                dep.t(lambda e, dc=dc, ts=ts, pb=pb: e.matmul(
                    bank(pb, 16), xn_own[:, dc, ts * 128:(ts + 1) * 128],
                    wr_bf[:, dc, :], start=(dc == 0), stop=(dc == NDC - 1)),
                    reads=[bA, bB], writes=[PSB[pb]])
            lg = fR[:, 0:16]
            dep.s(lambda e, pb=pb, lg=lg: e.activation(
                lg, bank(pb, 16), AF.Sigmoid),
                reads=[PSB[pb]], writes=[bF2])
            sg = fR[:, 16:32]
            dep.v(lambda e, lg=lg, sg=sg: e.tensor_add(sg, lg, ebias_bc[:, :]),
                  reads=[bF2, buf("ebias_bc")], writes=[bF2])
            t64 = fR[:, 32:96].rearrange("p (g a b) -> p g a b", g=4, a=4)
            t64b = fR[:, 96:160].rearrange("p (g a b) -> p g a b", g=4, a=4)
            sg_g = sg.rearrange("p (g a) -> p g a", g=4)
            m_self = sg_g.unsqueeze(3).broadcast_to([128, 4, 4, 4])
            m_comp = sg_g.unsqueeze(2).broadcast_to([128, 4, 4, 4])
            dep.v(lambda e, t64=t64, m_comp=m_comp, m_self=m_self:
                  e.tensor_tensor(t64, m_comp, m_self, ALU.is_gt),
                  reads=[bF2], writes=[bF2])
            dep.v(lambda e, t64b=t64b, m_comp=m_comp, m_self=m_self:
                  e.tensor_tensor(t64b, m_comp, m_self, ALU.is_equal),
                  reads=[bF2], writes=[bF2])
            lt4g = lt4_sb[:, :].rearrange("p (a b) -> p a b", a=4) \
                .unsqueeze(1).broadcast_to([128, 4, 4, 4])
            dep.v(lambda e, t64b=t64b, lt4g=lt4g:
                  e.tensor_tensor(t64b, t64b, lt4g, ALU.mult),
                  reads=[bF2, buf("lt4")], writes=[bF2])
            dep.v(lambda e, t64=t64, t64b=t64b:
                  e.tensor_tensor(t64, t64, t64b, ALU.add),
                  reads=[bF2], writes=[bF2])
            rk = fR[:, 160:176]
            dep.v(lambda e, rk=rk, t64=t64: e.reduce_sum(
                rk.rearrange("p (g a) -> p g a", g=4), t64, axis=AX.X),
                reads=[bF2], writes=[bF2])
            sel4 = fR[:, 176:192]
            dep.v(lambda e, sel4=sel4, rk=rk: e.tensor_single_scalar(
                sel4, rk, 1.5, ALU.is_lt),
                reads=[bF2], writes=[bF2])
            dep.v(lambda e, sg=sg, sel4=sel4: e.tensor_tensor(
                fR[:, 208:224], sg, sel4, ALU.mult),
                reads=[bF2], writes=[bF2])
            gsc = fR[:, 192:196]
            dep.v(lambda e, gsc=gsc: e.reduce_sum(
                gsc.rearrange("p (g o) -> p g o", g=4),
                fR[:, 208:224].rearrange("p (g a) -> p g a", g=4), axis=AX.X),
                reads=[bF2], writes=[bF2])
            g16 = fR[:, 32:48].rearrange("p (a b) -> p a b", a=4)
            g16b = fR[:, 48:64].rearrange("p (a b) -> p a b", a=4)
            gs_self = gsc.unsqueeze(2).broadcast_to([128, 4, 4])
            gs_comp = gsc.unsqueeze(1).broadcast_to([128, 4, 4])
            dep.v(lambda e, g16=g16, gs_comp=gs_comp, gs_self=gs_self:
                  e.tensor_tensor(g16, gs_comp, gs_self, ALU.is_gt),
                  reads=[bF2], writes=[bF2])
            dep.v(lambda e, g16b=g16b, gs_comp=gs_comp, gs_self=gs_self:
                  e.tensor_tensor(g16b, gs_comp, gs_self, ALU.is_equal),
                  reads=[bF2], writes=[bF2])
            lt4v = lt4_sb[:, :].rearrange("p (a b) -> p a b", a=4)
            dep.v(lambda e, g16b=g16b, lt4v=lt4v:
                  e.tensor_tensor(g16b, g16b, lt4v, ALU.mult),
                  reads=[bF2, buf("lt4")], writes=[bF2])
            dep.v(lambda e, g16=g16, g16b=g16b:
                  e.tensor_tensor(g16, g16, g16b, ALU.add),
                  reads=[bF2], writes=[bF2])
            rkg = fR[:, 196:200]
            dep.v(lambda e, rkg=rkg, g16=g16: e.reduce_sum(
                rkg.rearrange("p (g o) -> p g o", g=4), g16, axis=AX.X),
                reads=[bF2], writes=[bF2])
            selg = fR[:, 200:204]
            dep.v(lambda e, selg=selg, rkg=rkg: e.tensor_single_scalar(
                selg, rkg, 1.5, ALU.is_lt),
                reads=[bF2], writes=[bF2])
            smask = fR[:, 208:224].rearrange("p (g a) -> p g a", g=4)
            dep.v(lambda e, smask=smask, selg=selg: e.tensor_copy(
                smask, selg.rearrange("p (g o) -> p g o", g=4)
                .broadcast_to([128, 4, 4])),
                reads=[bF2], writes=[bF2])
            msk = fR[:, 224:240]
            dep.v(lambda e, msk=msk, sg=sg: e.tensor_tensor(
                msk, sg, fR[:, 208:224], ALU.mult),
                reads=[bF2], writes=[bF2])
            t256 = fR[:, 272:528].rearrange("p (a b) -> p a b", a=16)
            t256b = fR[:, 528:784].rearrange("p (a b) -> p a b", a=16)
            mm_self = msk.unsqueeze(2).broadcast_to([128, 16, 16])
            mm_comp = msk.unsqueeze(1).broadcast_to([128, 16, 16])
            dep.v(lambda e, t256=t256, mm_comp=mm_comp, mm_self=mm_self:
                  e.tensor_tensor(t256, mm_comp, mm_self, ALU.is_gt),
                  reads=[bF2], writes=[bF2])
            dep.v(lambda e, t256b=t256b, mm_comp=mm_comp, mm_self=mm_self:
                  e.tensor_tensor(t256b, mm_comp, mm_self, ALU.is_equal),
                  reads=[bF2], writes=[bF2])
            lt16v = lt16_sb[:, :].rearrange("p (a b) -> p a b", a=16)
            dep.v(lambda e, t256b=t256b, lt16v=lt16v:
                  e.tensor_tensor(t256b, t256b, lt16v, ALU.mult),
                  reads=[bF2, buf("lt16")], writes=[bF2])
            dep.v(lambda e, t256=t256, t256b=t256b:
                  e.tensor_tensor(t256, t256, t256b, ALU.add),
                  reads=[bF2], writes=[bF2])
            rk16 = fR[:, 160:176]
            dep.v(lambda e, rk16=rk16, t256=t256: e.reduce_sum(
                rk16.rearrange("p (a o) -> p a o", a=16), t256, axis=AX.X),
                reads=[bF2], writes=[bF2])
            sel = fR[:, 176:192]
            dep.v(lambda e, sel=sel, rk16=rk16: e.tensor_single_scalar(
                sel, rk16, 1.5, ALU.is_lt),
                reads=[bF2], writes=[bF2])
            w_ = fR[:, 240:256]
            dep.v(lambda e, w_=w_, lg=lg, sel=sel: e.tensor_tensor(
                w_, lg, sel, ALU.mult),
                reads=[bF2], writes=[bF2])
            dep.v(lambda e, w_=w_: e.reduce_sum(fR[:, 256:257], w_, axis=AX.X),
                  reads=[bF2], writes=[bF2])
            dep.v(lambda e: e.tensor_single_scalar(fR[:, 256:257], fR[:, 256:257],
                                               1e-20, ALU.add),
                  reads=[bF2], writes=[bF2])
            dep.v(lambda e: e.reciprocal(fR[:, 257:258], fR[:, 256:257]),
                  reads=[bF2], writes=[bF2])
            dep.v(lambda e, ts=ts, w_=w_: e.tensor_scalar(
                comb[:, ts * 16:(ts + 1) * 16], w_, fR[:, 257:258], None,
                ALU.mult),
                reads=[bF2], writes=[bCMB])
        dep.barrier()

        dep.phase("PD1")
        wup_v = Ba[:, :].rearrange("p (sl dc i) -> p sl dc i", sl=2, dc=NDC)
        bWU = [buf("wup0"), buf("wup1")]
        dsem("wup0"); dsem("wup1"); dsem("aw0"); dsem("aw1")
        aT_st = trst[:, 0:2048].rearrange("p (sl ic t) -> p sl ic t", sl=2, ic=8)
        bAT = [buf("aT0"), buf("aT1")]
        a_tok = f2[:, 0:512].bitcast(BF16).rearrange("p (sl x) -> p sl x", sl=2)
        rtmp2 = f2[:, 1024:2048].rearrange("p (sl x) -> p sl x", sl=2)

        def load_wup(e_, sl):
            src_ = wup_in[e_].rearrange("(dc p) i -> p dc i", p=128)
            dep.dma("g", "wup%d" % sl, lambda e, s=src_, sl=sl:
                    e.dma_start(out=wup_v[:, sl], in_=s),
                    writes=[bWU[sl]])

        load_wup(0, 0)
        for ex in range(NE):
            sl = ex % 2
            if ex + 1 < NE:
                load_wup(ex + 1, 1 - sl)
            for ts in range(16):
                for ih in range(2):
                    pb = ih
                    for dc in range(NDC):
                        dep.t(lambda e, dc=dc, ts=ts, ih=ih, pb=pb, sl=sl:
                              e.matmul(
                                  bank(pb),
                                  xn_own[:, dc, ts * 128:(ts + 1) * 128],
                                  wup_v[:, sl, dc, ih * 512:(ih + 1) * 512],
                                  start=(dc == 0), stop=(dc == NDC - 1)),
                              reads=[bA, bWU[sl]], writes=[PSB[pb]])
                for ih in range(2):
                    pb = ih
                    rt = rtmp2[:, pb, :]
                    cvec = comb[:, ts * 16 + ex:ts * 16 + ex + 1]
                    dep.s(lambda e, pb=pb, rt=rt, cvec=cvec: e.activation(
                        rt, bank(pb), AF.Relu, scale=cvec),
                        reads=[PSB[pb], bCMB], writes=[bRT[pb]])
                    atk = a_tok[:, pb, :]
                    dep.v(lambda e, pb=pb, rt=rt, atk=atk: e.tensor_tensor(
                        atk, bank(pb), rt, ALU.mult),
                        reads=[PSB[pb], bRT[pb]], writes=[bAQ[pb]])
                    tb = 2 + ih
                    for jj in range(4):
                        dep.t(lambda e, jj=jj, tb=tb, atk=atk: e.transpose(
                            bank_bf(tb)[:, jj * 128:(jj + 1) * 128],
                            atk[:, jj * 128:(jj + 1) * 128], idsb[:, :]),
                            reads=[bAQ[tb - 2], buf("idsb")], writes=[PSB[tb]])
                    tsl = ts % 2
                    dep.v(lambda e, tb=tb, ih=ih, tsl=tsl: e.tensor_copy(
                        aT_st[:, tsl, ih * 4:(ih + 1) * 4, :].rearrange(
                            "p a b -> p (a b)"),
                        bank_bf(tb)[:, 0:512]),
                        reads=[PSB[tb]], writes=[bAT[tsl]])
                tsl = ts % 2
                dst = a_hbm[ex, :, :, ts * 128:(ts + 1) * 128].rearrange(
                    "ic p t -> p ic t")
                dep.dma("q", "aw%d" % tsl, lambda e, d=dst, tsl=tsl: e.dma_start(
                    out=d, in_=aT_st[:, tsl, :, :]),
                    reads=[bAT[tsl]], writes=[HB["a"]])
        dep.barrier()

        dep.phase("PD2")
        acc = Ba[:, :].rearrange("p (ts d) -> p ts d", ts=16)
        wdn_v = A[:, :].rearrange("p (sl ic d) -> p sl ic d", sl=2, ic=8)
        bWD = [buf("wdn0"), buf("wdn1")]
        aT_ring = Dn[:, 0:4096].rearrange("p (r ic t) -> p r ic t", r=4, ic=8)
        bATr = [buf("atr%d" % r) for r in range(4)]
        dsem("wdn0"); dsem("wdn1")
        dsem("atr0"); dsem("atr1"); dsem("atr2"); dsem("atr3")

        def load_wdn(e_, sl):
            src_ = wdn_in[e_].rearrange("(ic p) d -> p ic d", p=128)
            dep.dma("g", "wdn%d" % sl, lambda e, s=src_, sl=sl:
                    e.dma_start(out=wdn_v[:, sl], in_=s),
                    writes=[bWD[sl]])

        def load_aT(e_, ts):
            r = ts % 4
            src = a_hbm[e_, :, :, ts * 128:(ts + 1) * 128].rearrange(
                "ic p t -> p ic t")
            dep.dma("q", "atr%d" % r, lambda e, s=src, d=aT_ring[:, r]:
                    e.dma_start(out=d, in_=s),
                    reads=[HB["a"]], writes=[bATr[r]])

        load_wdn(0, 0)
        load_aT(0, 0)
        load_aT(0, 1)
        for ex in range(NE):
            sl = ex % 2
            if ex + 1 < NE:
                load_wdn(ex + 1, 1 - sl)
            for ts in range(16):
                r = ts % 4
                if ts + 2 < 16:
                    load_aT(ex, ts + 2)
                elif ex + 1 < NE:
                    load_aT(ex + 1, ts + 2 - 16)
                pg = (ts % 2) * 4
                for dq in range(4):
                    for ic in range(8):
                        dep.t(lambda e, ic=ic, dq=dq, pg=pg, r=r, sl=sl:
                              e.matmul(
                                  bank(pg + dq), aT_ring[:, r, ic, :],
                                  wdn_v[:, sl, ic, dq * 512:(dq + 1) * 512],
                                  start=(ic == 0), stop=(ic == 7)),
                              reads=[bATr[r], bWD[sl]], writes=[PSB[pg + dq]])
                ps4 = PS[:, pg * 512:(pg + 4) * 512]
                psb4 = [PSB[pg + i] for i in range(4)]
                if ex == 0:
                    dep.v(lambda e, ps4=ps4, ts=ts: e.tensor_copy(
                        acc[:, ts, :], ps4),
                        reads=psb4, writes=[bB])
                else:
                    dep.v(lambda e, ps4=ps4, ts=ts: e.tensor_tensor(
                        acc[:, ts, :], ps4, acc[:, ts, :], ALU.add),
                        reads=psb4 + [bB], writes=[bB])
        dep.barrier()

        dep.phase("PE")
        wo_dq = A[:, 0:8192].rearrange("p (fc x) -> p fc x", fc=16)
        shd_dq = A[:, 8192:24576].rearrange("p (ic x) -> p ic x", ic=32)
        yst = A[:, 24576:26624].bitcast(F32).rearrange("p (sl x) -> p sl x",
                                                       sl=2)
        bWO, bSHD = buf("wo"), buf("shd")
        bYS = [buf("ys0"), buf("ys1")]
        ctx_ring = Dn[:, 0:8192].rearrange("p (r fc t) -> p r fc t", r=4, fc=16)
        asT_ring = Dn[:, 8192:16384].rearrange("p (r ic t) -> p r ic t",
                                               r=2, ic=32)
        bCXr = [buf("cxr%d" % r) for r in range(4)]
        bASr = [buf("asr%d" % r) for r in range(2)]
        bXr = [buf("xr0"), buf("xr1")]
        dep.dma("q", "bo", lambda e: e.dma_start(
            out=E[:, :], in_=bo_in[None, :].broadcast_to([128, D])),
            reads=[bE], writes=[bE])
        dsem("wo"); dsem("shd"); dsem("bo"); dsem("yw0"); dsem("yw1")
        for _r in range(4): dsem("cxr%d" % _r)
        for _r in range(2): dsem("asr%d" % _r)
        for _r in range(2): dsem("xr%d" % _r)
        dsem("yw")

        woT_r = woT.rearrange("(fc p) d -> p fc d", p=128)
        shdn_r = shdn_in.rearrange("(ic p) d -> p ic d", p=128)

        def load_pe_weights(dq):
            dep.dma("g", "wo", lambda e, dq=dq: e.dma_start(
                out=wo_dq[:, :, :],
                in_=woT_r[:, :, dq * 512:(dq + 1) * 512]), writes=[bWO])
            dep.dma("g", "shd", lambda e, dq=dq: e.dma_start(
                out=shd_dq[:, :, :],
                in_=shdn_r[:, :, dq * 512:(dq + 1) * 512]), writes=[bSHD])

        def load_pe_ts(ts):
            r, r2 = ts % 4, ts % 2
            src = ctx_hbm[:, :, ts * 128:(ts + 1) * 128].rearrange(
                "h p t -> p h t")
            dep.dma("q", "cxr%d" % r, lambda e, s=src, d=ctx_ring[:, r]:
                    e.dma_start(out=d, in_=s),
                    reads=[HB["ctx"]], writes=[bCXr[r]])
            src2 = as_hbm[:, :, ts * 128:(ts + 1) * 128].rearrange(
                "ic p t -> p ic t")
            dep.dma("q", "asr%d" % r2, lambda e, s=src2, d=asT_ring[:, r2]:
                    e.dma_start(out=d, in_=s),
                    reads=[HB["as"]], writes=[bASr[r2]])

        def load_pe_x(ts, dq):
            cs = ts % 2
            src = x_own[ts * 128:(ts + 1) * 128, dq * 512:(dq + 1) * 512]
            dep.dma("q", "xr%d" % cs, lambda e, s=src,
                    d=C[:, cs * 1024:cs * 1024 + 512]:
                    e.dma_start(out=d, in_=s), writes=[bXr[cs]])

        for dq in range(4):
            load_pe_weights(dq)
            load_pe_ts(0)
            load_pe_x(0, dq)
            for ts in range(16):
                r, r2, cs = ts % 4, ts % 2, ts % 2
                if ts + 1 < 16:
                    load_pe_ts(ts + 1)
                    load_pe_x(ts + 1, dq)
                pb = ts % 2
                for fc in range(16):
                    dep.t(lambda e, fc=fc, pb=pb, r=r: e.matmul(
                        bank(pb), ctx_ring[:, r, fc, :], wo_dq[:, fc, :],
                        start=(fc == 0), stop=False),
                        reads=[bCXr[r], bWO], writes=[PSB[pb]])
                for ic in range(32):
                    dep.t(lambda e, ic=ic, pb=pb, r2=r2: e.matmul(
                        bank(pb), asT_ring[:, r2, ic, :], shd_dq[:, ic, :],
                        start=False, stop=(ic == 31)),
                        reads=[bASr[r2], bSHD], writes=[PSB[pb]])
                ys = yst[:, cs, :]
                dep.v(lambda e, pb=pb, ys=ys, cs=cs: e.tensor_tensor(
                    ys, bank(pb), C[:, cs * 1024:cs * 1024 + 512], ALU.add),
                    reads=[PSB[pb], bXr[cs]], writes=[bYS[cs]])
                dep.v(lambda e, ys=ys, ts=ts, dq=dq: e.tensor_tensor(
                    ys, ys, acc[:, ts, dq * 512:(dq + 1) * 512], ALU.add),
                    reads=[bB, bYS[cs]], writes=[bYS[cs]])
                dep.v(lambda e, ys=ys, dq=dq: e.tensor_tensor(
                    ys, ys, E[:, dq * 512:(dq + 1) * 512], ALU.add),
                    reads=[bE, bYS[cs]], writes=[bYS[cs]])
                dst = y_out[ts * 128:(ts + 1) * 128, dq * 512:(dq + 1) * 512]
                dep.dma("q", "yw%d" % cs, lambda e, d=dst, s=ys:
                        e.dma_start(out=d, in_=s),
                        reads=[bYS[cs]], writes=[buf("yout")])

        for name, dst in tap_outs.items():
            src = taps[name]
            n0 = src.tensor.shape[0]
            for i in range(n0):
                sflat = src[i].rearrange(
                    " ".join("abc"[:len(src.tensor.shape) - 1]) + " -> ("
                    + " ".join("abc"[:len(src.tensor.shape) - 1]) + ")")
                dflat = dst[i].rearrange(
                    " ".join("abc"[:len(dst.tensor.shape) - 1]) + " -> ("
                    + " ".join("abc"[:len(dst.tensor.shape) - 1]) + ")")
                dep.dma("q", "yw", lambda e, s=sflat, d=dflat: e.dma_start(
                    out=d[None, :], in_=s[None, :]),
                    reads=[HB[tapbuf[name]]], writes=[buf("tap")])

        dep.barrier()

        with nc.Block() as block:
            allsems = [bar_sem, boot_sem] + list(sems.values()) \
                + list(dma_sems.values())
            nums = sorted(s.num for s in allsems)
            assert nums[0] == bar_sem.num
            assert nums[-1] - nums[0] + 1 == len(nums), "sems not contiguous"
            pad_aps = {"v": (ln1[:, 14:15], ln1[:, 15:16]),
                       "s": (ln1[:, 12:13], ln1[:, 13:14])}
            if os.environ.get("KDBG_NOPAD"):
                pad_aps = None
            dep.emit(block, sems, dma_sems, bar_sem, boot_sem,
                     range(nums[0], nums[-1] + 1), pad_aps, nc)

    return nc


_BUILT = {}


def _get_program(debug_taps=()):
    key = tuple(sorted(debug_taps))
    if key not in _BUILT:
        _BUILT[key] = build_program(key)
    return _BUILT[key]


def _own_rows(p):
    idx = np.arange(OWN)
    i, r = idx // 128, idx % 128
    return (2 * i + p) * 128 + r


def _cmask_np(p):
    w = np.arange(1024)[None, :]
    r = np.arange(128)[:, None]
    out = np.stack([np.where(w <= 256 * qt + 128 * p + r, 0.0, NEG)
                    for qt in range(4)])
    return out.astype(ml_dtypes.bfloat16)


def kernel(L0_attn_x, L0_attn_Wq, L0_attn_bq, L0_attn_Wk, L0_attn_bk,
           L0_attn_Wv, L0_attn_bv, L0_attn_Wo, L0_attn_bo,
           L0_ffn_router_weight, L0_ffn_e_bias, L0_ffn_w_up, L0_ffn_w_down,
           L0_ffn_shared_up, L0_ffn_shared_down, _debug_taps=()):
    f32 = np.float32
    x = np.ascontiguousarray(np.asarray(L0_attn_x, dtype=f32))
    shared = {
        "wqT": np.ascontiguousarray(np.asarray(L0_attn_Wq, dtype=f32).T),
        "wkT": np.ascontiguousarray(np.asarray(L0_attn_Wk, dtype=f32).T),
        "wvT": np.ascontiguousarray(np.asarray(L0_attn_Wv, dtype=f32).T),
        "woT": np.ascontiguousarray(np.asarray(L0_attn_Wo, dtype=f32).T),
        "wrT": np.ascontiguousarray(
            np.asarray(L0_ffn_router_weight, dtype=f32).T),
        "bq": np.asarray(L0_attn_bq, dtype=f32),
        "bk": np.asarray(L0_attn_bk, dtype=f32),
        "bv": np.asarray(L0_attn_bv, dtype=f32),
        "bo": np.asarray(L0_attn_bo, dtype=f32),
        "ebias": np.asarray(L0_ffn_e_bias, dtype=f32),
        "wup": np.ascontiguousarray(np.asarray(L0_ffn_w_up, dtype=f32)),
        "wdn": np.ascontiguousarray(np.asarray(L0_ffn_w_down, dtype=f32)),
        "shup": np.ascontiguousarray(np.asarray(L0_ffn_shared_up, dtype=f32)),
        "shdn": np.ascontiguousarray(np.asarray(L0_ffn_shared_down, dtype=f32)),
    }
    cmasks = [_cmask_np(0), _cmask_np(1)]
    rows = [_own_rows(0), _own_rows(1)]
    nc = _get_program(_debug_taps)
    in_maps = []
    for c in range(8):
        b, p = c // 2, c % 2
        m = dict(shared)
        m["x_full"] = x[b]
        m["x_own"] = np.ascontiguousarray(x[b][rows[p]])
        m["cmask"] = cmasks[p]
        in_maps.append(m)
    ncore = int(os.environ.get("KDBG_CORES", "8"))
    res = bass_utils.run_bass_kernel_spmd(nc, in_maps[:ncore],
                                          core_ids=list(range(ncore)))
    out = np.zeros((B, S, D), dtype=f32)
    for c in range(ncore):
        b, p = c // 2, c % 2
        out[b][rows[p]] = res.results[c]["y"]
    if _debug_taps:
        return out, res
    return out
